# version 01
# baseline (speedup 1.0000x reference)
import numpy as np
import ml_dtypes
import concourse.bass as bass
import concourse.mybir as mybir
from concourse.bass_utils import run_bass_kernel_spmd

F32 = mybir.dt.float32
BF16 = mybir.dt.bfloat16
AF = mybir.ActivationFunctionType
ALU = mybir.AluOpType
AX = mybir.AxisListType

BF = ml_dtypes.bfloat16
NCORES = 8
T_CORE = 1024
NTILE = 8
D = 2048
NDC = 16

ENGINES = ("sync", "scalar", "vector", "tensor", "gpsimd")


class Sem:
    __slots__ = ("h", "count")
    def __init__(self, h):
        self.h = h
        self.count = 0


class Prog:
    def __init__(self, nc):
        self.nc = nc
        self.streams = {e: [] for e in ENGINES}
        self._semi = 0

    def sem(self, name=None):
        self._semi += 1
        return Sem(self.nc.alloc_semaphore(name or f"s{self._semi}"))

    def on(self, eng, fn):
        self.streams[eng].append(fn)

    def wait(self, eng, sem, val=None):
        v = sem.count if val is None else val
        if v > 0:
            self.on(eng, lambda e, h=sem.h, v=v: e.wait_ge(h, v))

    def emit(self, eng, fn, sem=None, inc=1):
        if sem is not None:
            sem.count += inc
            self.on(eng, lambda e, fn=fn, h=sem.h, k=inc: fn(e).then_inc(h, k))
            return sem.count
        self.on(eng, fn)
        return None

    def dma(self, eng, out, in_, sem, **kw):
        sem.count += 16
        self.on(eng, lambda e, o=out, i=in_, h=sem.h, kw=kw:
                e.dma_start(out=o, in_=i, **kw).then_inc(h, 16))
        return sem.count

    def build_block(self):
        with self.nc.Block() as blk:
            for eng in ENGINES:
                fns = self.streams[eng]
                if not fns:
                    continue
                def runner(e, fns=fns):
                    for fn in fns:
                        fn(e)
                getattr(blk, eng)(runner)


class Stream:
    def __init__(self, p, nbuf=2):
        self.p = p
        self.sems = [p.sem() for _ in range(nbuf)]
        self.i = 0

    def dma(self, eng, out, in_, gate=None, slot=None, **kw):
        if slot is None:
            slot = self.i % len(self.sems)
            self.i += 1
        s = self.sems[slot]
        if gate is not None:
            gates = gate if isinstance(gate, list) else [gate]
            for g in gates:
                self.p.wait(eng, g[0], g[1])
        self.p.dma(eng, out, in_, s, **kw)
        return (s, s.count)


class TP:
    def __init__(self, p, ident_sb, nbanks=2):
        self.p = p
        self.ident = ident_sb
        self.banks = [p.nc.alloc_psum_tensor(f"ptp{i}", [128, 512], BF16)
                      for i in range(nbanks)]
        self.rel = [0] * nbanks
        self.i = 0
        self.s_tp = p.sem("s_tp")
        self.s_ev = p.sem("s_ev")

    def group(self, srcs, dsts, src_ready=None):
        p = self.p
        b = self.i % len(self.banks)
        self.i += 1
        p.wait("tensor", self.s_ev, self.rel[b])
        if src_ready is not None:
            p.wait("tensor", src_ready[0], src_ready[1])
        bank = self.banks[b]
        n = len(srcs)
        for j, src in enumerate(srcs):
            w = src.shape[0]
            p.emit("tensor",
                   lambda e, o=bank[0:src.shape[1], j * 128: j * 128 + w], i=src,
                   idn=self.ident[0:w, 0:w]: e.transpose(o, i, idn),
                   self.s_tp if j == n - 1 else None)
        p.wait("vector", self.s_tp, self.s_tp.count)
        for j, (s, d) in enumerate(zip(srcs, dsts)):
            w = s.shape[0]
            p.emit("vector",
                   lambda e, o=d, i=bank[:, j * 128: j * 128 + w]: e.tensor_copy(o, i),
                   self.s_ev if j == n - 1 else None)
        self.rel[b] = self.s_ev.count


def core_strips(core):
    return (core, 15 - core)


class LNBufs:
    def __init__(self, nc, tag):
        self.x = nc.alloc_sbuf_tensor(f"lnx{tag}", [128, 2 * 2048], F32)
        self.stats = nc.alloc_sbuf_tensor(f"lnst{tag}", [128, 24], F32)
        self.mv = nc.alloc_sbuf_tensor(f"lnmv{tag}", [128, 2], F32)
        self.rstd = nc.alloc_sbuf_tensor(f"lnrs{tag}", [128, 1], F32)
        self.tmp = nc.alloc_sbuf_tensor(f"lntp{tag}", [128, 1], F32)


def emit_ln(p, lb, x_dram, s_x, s_v, s_sc, u_tok, tp=None, u_fm=None,
            u_tiles=NTILE):
    xs = Stream(p, 2)
    xrel = {0: 0, 1: 0}
    urel = {0: 0, 1: 0}
    for t in range(NTILE):
        buf = t % 2
        xv = lb.x[:, buf * 2048:(buf + 1) * 2048]
        xw = xs.dma("sync", xv, x_dram[t * 128:(t + 1) * 128, :],
                    gate=(s_v, xrel[buf]), slot=buf)
        p.wait("vector", xw[0], xw[1])
        for i in range(3):
            p.emit("vector", lambda e, o=lb.stats[:, i * 6:(i + 1) * 6],
                   x=xv[:, i * 512:(i + 1) * 512]: e.bn_stats(o, x))
        p.emit("vector", lambda e, o=lb.stats[:, 18:24], x=xv[:, 1536:2048]:
               e.bn_stats(o, x), s_v)
        p.wait("vector", s_v)
        p.emit("vector", lambda e, o=lb.mv, i=lb.stats: e.bn_aggr(o[:], i[:]), s_v)
        p.wait("vector", s_v)
        p.emit("vector", lambda e, o=lb.tmp, i=lb.mv: e.tensor_scalar(
            out=o[:], in0=i[:, 1:2], scalar1=1e-5, scalar2=None,
            op0=ALU.add, op1=ALU.bypass), s_v)
        p.wait("vector", s_v)
        p.emit("vector", lambda e, o=lb.rstd, i=lb.tmp: e.reciprocal(o[:], i[:]), s_v)
        p.wait("scalar", s_v, s_v.count)
        p.emit("scalar", lambda e, o=lb.rstd: e.activation(
            out=o[:], in_=o[:], func=AF.Sqrt), s_sc)
        p.wait("vector", s_sc, s_sc.count)
        ut = t % u_tiles
        uv = u_tok[:, ut * 2048:(ut + 1) * 2048]
        if u_tiles == 2:
            p.wait("vector", tp.s_tp, urel[ut])
        u_done = p.emit("vector", lambda e, o=uv, x=xv, m=lb.mv, r=lb.rstd:
                        e.tensor_scalar(out=o, in0=x, scalar1=m[:, 0:1], scalar2=r[:],
                                        op0=ALU.subtract, op1=ALU.mult), s_v)
        xrel[buf] = u_done
        if tp is not None:
            for g in range(4):
                srcs = [uv[:, (g * 4 + j) * 128:(g * 4 + j + 1) * 128] for j in range(4)]
                dsts = [u_fm[:, (g * 4 + j) * 1024 + t * 128:
                             (g * 4 + j) * 1024 + (t + 1) * 128] for j in range(4)]
                tp.group(srcs, dsts, src_ready=(s_v, u_done))
            urel[ut] = tp.s_tp.count


def emit_ffn0(p, nc, x_dram, xo_dram, wg, wu, wd, ident_sb, tag, coef=0.5):
    s_x = p.sem(); s_v = p.sem(); s_sc = p.sem()
    s_mm = p.sem(); s_out = p.sem()
    wn_s = Stream(p, 2)
    wd_s = Stream(p, 2)
    tp = TP(p, ident_sb)
    lb = LNBufs(nc, tag)
    u_tok = nc.alloc_sbuf_tensor(f"utok{tag}", [128, 2 * 2048], BF16)
    u_fm = nc.alloc_sbuf_tensor(f"ufm{tag}", [128, 16 * 1024], BF16)
    wnat = nc.alloc_sbuf_tensor(f"wnat{tag}", [128, 4 * 2048], BF16)
    wfm = nc.alloc_sbuf_tensor(f"wfm{tag}", [128, 4 * 2048], BF16)
    silu_sb = nc.alloc_sbuf_tensor(f"silu{tag}", [128, 2 * 512], BF16)
    gated = nc.alloc_sbuf_tensor(f"gated{tag}", [128, 8 * 1024], BF16)
    wdnat = nc.alloc_sbuf_tensor(f"wdnat{tag}", [128, 2 * 4 * 1024], BF16)
    wdfm = nc.alloc_sbuf_tensor(f"wdfm{tag}", [128, 2 * 8 * 512], BF16)
    acc = nc.alloc_sbuf_tensor(f"acc{tag}", [128, 8 * 2048], F32)

    pga = [nc.alloc_psum_tensor(f"pga{tag}{i}", [128, 512], F32) for i in range(4)]
    pdn = [nc.alloc_psum_tensor(f"pdn{tag}{i}", [128, 512], F32) for i in range(2)]
    pga_rel = [0] * 4
    pdn_rel = [0] * 2

    emit_ln(p, lb, x_dram, s_x, s_v, s_sc, u_tok, tp=tp, u_fm=u_fm, u_tiles=2)
    u_fm_ready = tp.s_ev.count

    wnat_rel = {}
    wfm_rel = {}
    wdnat_rel = {}
    wdfm_rel = {}

    def dma_w(w, g):
        buf = g % 2
        src = (wg if w == 0 else wu)[g * 128:(g + 1) * 128, :]
        dst = wnat[:, (w * 2 + buf) * 2048:(w * 2 + buf + 1) * 2048]
        return wn_s.dma("gpsimd", dst, src,
                        gate=(tp.s_tp, wnat_rel.get((w, buf), 0)), slot=buf)

    wn_after = {}
    for g in (0, 1):
        for w in (0, 1):
            wn_after[g] = dma_w(w, g)

    gated_rel = 0
    ga_i = 0

    for fb in range(8):
        for fc in range(8):
            g = fb * 8 + fc
            buf = g % 2
            for w in (0, 1):
                nat = wnat[:, (w * 2 + buf) * 2048:(w * 2 + buf + 1) * 2048]
                fm = wfm[:, (w * 2 + buf) * 2048:(w * 2 + buf + 1) * 2048]
                p.wait("tensor", s_mm, wfm_rel.get((w, buf), 0))
                for grp in range(4):
                    srcs = [nat[:, (grp * 4 + j) * 128:(grp * 4 + j + 1) * 128]
                            for j in range(4)]
                    dsts = [fm[:, (grp * 4 + j) * 128:(grp * 4 + j + 1) * 128]
                            for j in range(4)]
                    tp.group(srcs, dsts, src_ready=wn_after[g])
                wnat_rel[(w, buf)] = tp.s_tp.count
            wfm_ready = tp.s_ev.count
            if g + 2 < 64:
                for w in (0, 1):
                    wn_after[g + 2] = dma_w(w, g + 2)
            p.wait("tensor", tp.s_ev, max(wfm_ready, u_fm_ready))
            for th in range(2):
                bank_g = pga[ga_i % 4]; ga_i += 1
                bank_u = pga[ga_i % 4]; ga_i += 1
                p.wait("tensor", s_sc, pga_rel[pga.index(bank_g)])
                for dc in range(NDC):
                    lhs = wfm[:, (0 * 2 + buf) * 2048 + dc * 128:
                              (0 * 2 + buf) * 2048 + (dc + 1) * 128]
                    rhs = u_fm[:, dc * 1024 + th * 512: dc * 1024 + (th + 1) * 512]
                    p.emit("tensor", lambda e, o=bank_g, l=lhs, r=rhs, s=(dc == 0),
                           t2=(dc == NDC - 1): e.matmul(o[:], l, r, start=s, stop=t2),
                           s_mm if dc == NDC - 1 else None)
                gate_done = s_mm.count
                p.wait("tensor", s_v, pga_rel[pga.index(bank_u)])
                for dc in range(NDC):
                    lhs = wfm[:, (1 * 2 + buf) * 2048 + dc * 128:
                              (1 * 2 + buf) * 2048 + (dc + 1) * 128]
                    rhs = u_fm[:, dc * 1024 + th * 512: dc * 1024 + (th + 1) * 512]
                    p.emit("tensor", lambda e, o=bank_u, l=lhs, r=rhs, s=(dc == 0),
                           t2=(dc == NDC - 1): e.matmul(o[:], l, r, start=s, stop=t2),
                           s_mm if dc == NDC - 1 else None)
                up_done = s_mm.count
                p.wait("scalar", s_mm, gate_done)
                sl = silu_sb[:, th * 512:(th + 1) * 512]
                silu_done = p.emit("scalar", lambda e, o=sl, i=bank_g:
                                   e.activation(out=o, in_=i[:], func=AF.Silu), s_sc)
                pga_rel[pga.index(bank_g)] = silu_done
                p.wait("vector", s_mm, up_done)
                p.wait("vector", s_sc, silu_done)
                if fc == 0 and th == 0:
                    p.wait("vector", s_mm, gated_rel)
                gv = gated[:, fc * 1024 + th * 512: fc * 1024 + (th + 1) * 512]
                gdone = p.emit("vector", lambda e, o=gv, a=sl, b=bank_u:
                               e.tensor_tensor(out=o, in0=a, in1=b[:], op=ALU.mult), s_v)
                pga_rel[pga.index(bank_u)] = gdone
        gated_ready = s_v.count

        for dc4 in range(4):
            buf = dc4 % 2
            for ds in range(4):
                src = wd[dc4 * 512 + ds * 128: dc4 * 512 + (ds + 1) * 128,
                         fb * 1024:(fb + 1) * 1024]
                dst = wdnat[:, (buf * 4 + ds) * 1024:(buf * 4 + ds + 1) * 1024]
                wd_loaded = wd_s.dma(
                    "gpsimd", dst, src, slot=buf,
                    gate=(tp.s_tp, wdnat_rel.get(buf, 0)) if ds == 0 else None)
            p.wait("tensor", s_mm, wdfm_rel.get(buf, 0))
            for fc in range(8):
                srcs = [wdnat[:, (buf * 4 + ds) * 1024 + fc * 128:
                              (buf * 4 + ds) * 1024 + (fc + 1) * 128] for ds in range(4)]
                dsts = [wdfm[:, buf * 4096 + fc * 512 + ds * 128:
                             buf * 4096 + fc * 512 + (ds + 1) * 128] for ds in range(4)]
                tp.group(srcs, dsts, src_ready=wd_loaded)
            wdnat_rel[buf] = tp.s_tp.count
            wdfm_ready = tp.s_ev.count
            p.wait("tensor", tp.s_ev, wdfm_ready)
            p.wait("tensor", s_v, gated_ready)
            for tt in range(8):
                bank = pdn[tt % 2]
                p.wait("tensor", s_v, pdn_rel[tt % 2])
                for fc in range(8):
                    lhs = gated[:, fc * 1024 + tt * 128: fc * 1024 + (tt + 1) * 128]
                    rhs = wdfm[:, buf * 4096 + fc * 512: buf * 4096 + (fc + 1) * 512]
                    p.emit("tensor", lambda e, o=bank, l=lhs, r=rhs, s=(fc == 0),
                           t2=(fc == 7): e.matmul(o[:], l, r, start=s, stop=t2),
                           s_mm if fc == 7 else None)
                dn_done = s_mm.count
                wdfm_rel[buf] = dn_done
                p.wait("vector", s_mm, dn_done)
                av = acc[:, tt * 2048 + dc4 * 512: tt * 2048 + (dc4 + 1) * 512]
                if fb == 0:
                    rel = p.emit("vector", lambda e, o=av, i=bank:
                                 e.tensor_copy(o, i[:]), s_v)
                else:
                    rel = p.emit("vector", lambda e, o=av, i=bank:
                                 e.tensor_tensor(out=o, in0=o, in1=i[:], op=ALU.add), s_v)
                pdn_rel[tt % 2] = rel
        gated_rel = s_mm.count

    out_rel = {0: 0, 1: 0}
    for t in range(NTILE):
        buf = t % 2
        xv = lb.x[:, buf * 2048:(buf + 1) * 2048]
        p.wait("sync", s_v, s_v.count)
        p.wait("sync", s_out, out_rel[buf])
        p.dma("sync", xv, x_dram[t * 128:(t + 1) * 128, :], s_x)
        p.wait("vector", s_x, s_x.count)
        p.emit("vector", lambda e, o=xv, a=acc[:, t * 2048:(t + 1) * 2048], x=xv, c=coef:
               e.scalar_tensor_tensor(out=o, in0=a, scalar=c, in1=x,
                                      op0=ALU.mult, op1=ALU.add), s_v)
        p.wait("scalar", s_v, s_v.count)
        p.dma("scalar", xo_dram[t * 128:(t + 1) * 128, :], xv, s_out)
        out_rel[buf] = s_out.count
    p.wait("scalar", s_out, s_out.count)


def emit_residual_out(p, lb, acc, x_dram, xo_dram, coef, s_x, s_v, s_out):
    xs = Stream(p, 2)
    out_rel = {0: 0, 1: 0}
    for t in range(NTILE):
        buf = t % 2
        xv = lb.x[:, buf * 2048:(buf + 1) * 2048]
        xw = xs.dma("sync", xv, x_dram[t * 128:(t + 1) * 128, :],
                    gate=[(s_v, s_v.count), (s_out, out_rel[buf])], slot=buf)
        p.wait("vector", xw[0], xw[1])
        p.emit("vector", lambda e, o=xv, a=acc[:, t * 2048:(t + 1) * 2048], x=xv, c=coef:
               e.scalar_tensor_tensor(out=o, in0=a, scalar=c, in1=x,
                                      op0=ALU.mult, op1=ALU.add), s_v)
        p.wait("scalar", s_v, s_v.count)
        p.dma("scalar", xo_dram[t * 128:(t + 1) * 128, :], xv, s_out)
        out_rel[buf] = s_out.count
    p.wait("scalar", s_out, s_out.count)


def emit_moe(p, nc, x_dram, xo_dram, W, ident_sb, ident32_sb, tag, coef=0.5, dbg=None):
    s_x = p.sem(); s_v = p.sem(); s_sc = p.sem()
    s_mm = p.sem(); s_out = p.sem()
    s_cw = p.sem()
    w1_s = Stream(p, 2)
    w2_s = Stream(p, 1)
    b2_s = Stream(p, 1)
    tp = TP(p, ident_sb)
    lb = LNBufs(nc, tag)
    u_tok = nc.alloc_sbuf_tensor(f"utok{tag}", [128, 2 * 2048], BF16)
    u_fm = nc.alloc_sbuf_tensor(f"ufm{tag}", [128, 16 * 1024], BF16)
    acc = nc.alloc_sbuf_tensor(f"acc{tag}", [128, 8 * 2048], F32)
    h_sb = nc.alloc_sbuf_tensor(f"h{tag}", [128, 11 * 1024], BF16)
    w2_sb = nc.alloc_sbuf_tensor(f"w2{tag}", [128, 11 * 2048], BF16)
    w1col = nc.alloc_sbuf_tensor(f"w1c{tag}", [128, 2 * 2048], BF16)
    b1cols = nc.alloc_sbuf_tensor(f"b1c{tag}", [128, 11 * 18], F32)
    b2buf = nc.alloc_sbuf_tensor(f"b2b{tag}", [1, 2048], BF16)
    b1nat = nc.alloc_sbuf_tensor(f"b1n{tag}", [18, 1408], F32)
    _ut_addr = nc.lookup_mls(u_tok).memorylocations[0].addr
    centnat = nc.alloc_sbuf_tensor_at(f"cn{tag}", [16, 2048], BF16, offset=_ut_addr)
    cent_fm = nc.alloc_sbuf_tensor(f"cf{tag}", [128, 16 * 16], BF16)
    biasb = nc.alloc_sbuf_tensor(f"bb{tag}", [128, 16], F32)
    bias1 = nc.alloc_sbuf_tensor(f"b1r{tag}", [1, 16], F32)
    ones1 = nc.alloc_sbuf_tensor(f"on{tag}", [1, 128], BF16)
    ones1f = nc.alloc_sbuf_tensor(f"onf{tag}", [1, 128], F32)
    onescol = nc.alloc_sbuf_tensor(f"oc{tag}", [128, 1], F32)
    g_sb = nc.alloc_sbuf_tensor(f"g{tag}", [128, 8 * 16], F32)
    sc_sb = nc.alloc_sbuf_tensor(f"sc{tag}", [128, 16], F32)
    m_sb = nc.alloc_sbuf_tensor(f"m{tag}", [128, 1], F32)
    msk_sb = nc.alloc_sbuf_tensor(f"mk{tag}", [128, 16], F32)
    sum_sb = nc.alloc_sbuf_tensor(f"su{tag}", [128, 1], F32)

    p_a = [nc.alloc_psum_tensor(f"pa{tag}{i}", [128, 512], F32) for i in range(2)]
    p_b = [nc.alloc_psum_tensor(f"pb{tag}{i}", [128, 512], F32) for i in range(2)]
    p_t32 = nc.alloc_psum_tensor(f"pt32{tag}", [128, 128], F32)
    pa_rel = [0, 0]
    pb_rel = [0, 0]

    p.emit("vector", lambda e: e.memset(ones1[:], 1.0), s_v)
    p.emit("vector", lambda e: e.memset(ones1f[:], 1.0), s_v)
    p.emit("vector", lambda e: e.memset(onescol[:], 1.0), s_v)
    const_ready = s_v.count
    p.dma("gpsimd", b1nat[0:2, :], W["sb1"][:], s_cw)
    p.dma("gpsimd", b1nat[2:18, :], W["rb1"][:], s_cw)
    p.dma("gpsimd", centnat[:], W["cent"][:], s_cw)
    p.dma("sync", bias1[:], W["bias"][None, :], s_x)
    b_loaded = s_cw.count
    bias_loaded = s_x.count

    p.wait("tensor", s_cw, b_loaded)
    t32_rel = 0
    for fc in range(11):
        p.wait("tensor", s_v, t32_rel)
        p.emit("tensor", lambda e, i=b1nat[0:18, fc * 128:(fc + 1) * 128],
               o=p_t32[0:128, 0:18], idn=ident32_sb:
               e.transpose(o, i, idn[0:18, 0:18]), s_mm)
        p.wait("vector", s_mm, s_mm.count)
        t32_rel = p.emit("vector", lambda e, o=b1cols[:, fc * 18:(fc + 1) * 18],
                         i=p_t32[0:128, 0:18]: e.tensor_copy(o, i), s_v)
    p.wait("tensor", s_x, bias_loaded)
    p.wait("tensor", s_v, max(const_ready, t32_rel))
    p.emit("tensor", lambda e, o=p_t32[0:128, 0:16], l=ones1f, r=bias1:
           e.matmul(o, l[:], r[:], start=True, stop=True), s_mm)
    p.wait("vector", s_mm, s_mm.count)
    p.emit("vector", lambda e, o=biasb, i=p_t32[0:128, 0:16]: e.tensor_copy(o[:], i), s_v)
    for dc in range(16):
        tp.group([centnat[0:16, dc * 128:(dc + 1) * 128]],
                 [cent_fm[:, dc * 16:(dc + 1) * 16]], src_ready=(s_cw, b_loaded))

    emit_ln(p, lb, x_dram, s_x, s_v, s_sc, u_tok, tp=tp, u_fm=u_fm, u_tiles=2)
    u_fm_ready = tp.s_ev.count

    p.wait("tensor", tp.s_ev, u_fm_ready)
    sc_rel = 0
    for tt in range(NTILE):
        bank = p_a[tt % 2]
        p.wait("tensor", s_sc, pa_rel[tt % 2])
        for dc in range(16):
            lhs = u_fm[:, dc * 1024 + tt * 128: dc * 1024 + (tt + 1) * 128]
            rhs = cent_fm[:, dc * 16:(dc + 1) * 16]
            p.emit("tensor", lambda e, o=bank[:, 0:16], l=lhs, r=rhs,
                   s=(dc == 0), t2=(dc == 15): e.matmul(o, l, r, start=s, stop=t2),
                   s_mm if dc == 15 else None)
        p.wait("scalar", s_mm, s_mm.count)
        p.wait("scalar", s_v, sc_rel)
        pa_rel[tt % 2] = p.emit("scalar", lambda e, o=sc_sb[:], i=bank[:, 0:16]:
                                e.activation(out=o, in_=i, func=AF.Sigmoid), s_sc)
        p.wait("vector", s_sc, s_sc.count)
        gv = g_sb[:, tt * 16:(tt + 1) * 16]
        p.emit("vector", lambda e, o=sc_sb, i=biasb: e.tensor_tensor(
            out=o[:], in0=o[:], in1=i[:], op=ALU.add), s_v)
        p.wait("vector", s_v)
        p.emit("vector", lambda e, o=gv, i=sc_sb: e.tensor_copy(o, i[:]), s_v)
        p.wait("vector", s_v)
        for it in range(3):
            p.emit("vector", lambda e, o=m_sb, i=gv: e.reduce_max(o[:], i, axis=AX.X), s_v)
            p.wait("vector", s_v)
            p.emit("vector", lambda e, o=msk_sb, i=gv, m=m_sb: e.tensor_scalar(
                out=o[:], in0=i, scalar1=m[:], scalar2=None,
                op0=ALU.is_ge, op1=ALU.bypass), s_v)
            p.wait("vector", s_v)
            p.emit("vector", lambda e, o=gv, i=msk_sb, g2=gv: e.scalar_tensor_tensor(
                out=o, in0=i[:], scalar=-1e30, in1=g2, op0=ALU.mult, op1=ALU.add), s_v)
            p.wait("vector", s_v)
        p.emit("vector", lambda e, o=m_sb, i=gv: e.reduce_max(o[:], i, axis=AX.X), s_v)
        p.wait("vector", s_v)
        p.emit("vector", lambda e, o=msk_sb, i=sc_sb, m=m_sb: e.tensor_scalar(
            out=o[:], in0=i[:], scalar1=m[:], scalar2=None,
            op0=ALU.is_ge, op1=ALU.bypass), s_v)
        p.wait("vector", s_v)
        p.emit("vector", lambda e, o=gv, a=msk_sb, b=sc_sb: e.tensor_tensor(
            out=o, in0=a[:], in1=b[:], op=ALU.mult), s_v)
        sc_rel = s_v.count
        p.wait("vector", s_v)
        p.emit("vector", lambda e, o=sum_sb, i=gv: e.reduce_sum(o[:], i, axis=AX.X), s_v)
        p.wait("vector", s_v)
        p.emit("vector", lambda e, o=sum_sb: e.tensor_scalar_max(o[:], o[:], 1e-8), s_v)
        p.wait("vector", s_v)
        p.emit("vector", lambda e, o=sum_sb: e.reciprocal(o[:], o[:]), s_v)
        p.wait("vector", s_v)
        p.emit("vector", lambda e, o=gv, m=sum_sb: e.tensor_scalar(
            out=o, in0=o, scalar1=m[:], scalar2=None, op0=ALU.mult, op1=ALU.bypass), s_v)

    if dbg is not None and "g" in dbg:
        p.wait("scalar", s_v, s_v.count)
        for tt in range(NTILE):
            p.dma("scalar", dbg["g"][tt * 128:(tt + 1) * 128, :],
                  g_sb[:, tt * 16:(tt + 1) * 16], s_out)
    for t in range(NTILE):
        for g in range(4):
            srcs = [u_fm[:, (g * 4 + j) * 1024 + t * 128:
                         (g * 4 + j) * 1024 + (t + 1) * 128] for j in range(4)]
            dsts = [acc[:, t * 2048 + (g * 4 + j) * 128:
                        t * 2048 + (g * 4 + j + 1) * 128] for j in range(4)]
            tp.group(srcs, dsts, src_ready=(tp.s_ev, u_fm_ready))
    acc_ready = tp.s_ev.count
    if dbg is not None and "u" in dbg:
        p.wait("scalar", tp.s_ev, acc_ready)
        for tt in range(NTILE):
            p.dma("scalar", dbg["u"][tt * 128:(tt + 1) * 128, :],
                  acc[:, tt * 2048:(tt + 1) * 2048], s_out)

    w1_rel = {0: 0, 1: 0}
    w2_rel = 0
    b2_rel = {0: 0, 1: 0}
    srcs_w1 = [W["sw1"][0], W["sw1"][1]] + [W["rw1"][j] for j in range(16)]
    srcs_w2 = [W["sw2"][0], W["sw2"][1]] + [W["rw2"][j] for j in range(16)]

    w1_ctr = [0]

    def load_w1col(j, fc):
        buf = w1_ctr[0] % 2
        w1_ctr[0] += 1
        src = srcs_w1[j][:, fc * 128:(fc + 1) * 128].rearrange(
            "(dc p) f -> p dc f", p=128)
        dst = w1col[:, buf * 2048:(buf + 1) * 2048].rearrange(
            "p (dc f) -> p dc f", dc=16)
        w = w1_s.dma("gpsimd", dst, src, gate=(s_mm, w1_rel[buf]), slot=buf)
        return (w, buf)

    w1_loaded = {(0, 0): load_w1col(0, 0)}
    for j in range(18):
        src_b2 = (W["sb2"][j] if j < 2 else W["rb2"][j - 2])
        b2_loaded = b2_s.dma("gpsimd", b2buf[:], src_b2[None, :],
                             gate=(s_mm, b2_rel[0]), slot=0)
        w2_loaded = w2_s.dma("gpsimd",
                             w2_sb[:].rearrange("p (fc d) -> p fc d", fc=11),
                             srcs_w2[j].rearrange("(fc p) d -> p fc d", p=128),
                             gate=(s_mm, w2_rel), slot=0)
        for fc in range(11):
            if fc + 1 < 11:
                w1_loaded[(j, fc + 1)] = load_w1col(j, fc + 1)
            elif j + 1 < 18:
                w1_loaded[(j + 1, 0)] = load_w1col(j + 1, 0)
            w1_cnt, buf = w1_loaded[(j, fc)]
            p.wait("tensor", w1_cnt[0], w1_cnt[1])
            for th in range(2):
                bank = p_a[th]
                p.wait("tensor", s_sc, pa_rel[th])
                for dc in range(16):
                    lhs = w1col[:, buf * 2048 + dc * 128: buf * 2048 + (dc + 1) * 128]
                    rhs = u_fm[:, dc * 1024 + th * 512: dc * 1024 + (th + 1) * 512]
                    p.emit("tensor", lambda e, o=bank, l=lhs, r=rhs, s=(dc == 0),
                           t2=(dc == 15): e.matmul(o[:], l, r, start=s, stop=t2),
                           s_mm if dc == 15 else None)
                if th == 1:
                    w1_rel[buf] = s_mm.count
                p.wait("scalar", s_mm, s_mm.count)
                hv = h_sb[:, fc * 1024 + th * 512: fc * 1024 + (th + 1) * 512]
                pa_rel[th] = p.emit("scalar", lambda e, o=hv, i=bank,
                                    b=b1cols[:, fc * 18 + j: fc * 18 + j + 1]:
                                    e.activation(out=o, in_=i[:], func=AF.Relu, bias=b),
                                    s_sc)
        h_ready = s_sc.count
        p.wait("tensor", w2_loaded[0], w2_loaded[1])
        p.wait("tensor", b2_loaded[0], b2_loaded[1])
        p.wait("tensor", s_sc, h_ready)
        if j == 0:
            p.wait("tensor", s_v, const_ready)
        for tt in range(NTILE):
            for d4 in range(4):
                bi = (tt * 4 + d4) % 2
                bank = p_b[bi]
                p.wait("tensor", s_v, pb_rel[bi])
                for fc in range(11):
                    lhs = h_sb[:, fc * 1024 + tt * 128: fc * 1024 + (tt + 1) * 128]
                    rhs = w2_sb[:, fc * 2048 + d4 * 512: fc * 2048 + (d4 + 1) * 512]
                    p.emit("tensor", lambda e, o=bank, l=lhs, r=rhs, s=(fc == 0),
                           t2=(fc == 10): e.matmul(o[:], l, r, start=s, stop=t2),
                           s_mm if fc == 10 else None)
                mm_done = s_mm.count
                p.wait("vector", s_mm, mm_done)
                if j == 0 and tt == 0 and d4 == 0:
                    p.wait("vector", tp.s_ev, acc_ready)
                av = acc[:, tt * 2048 + d4 * 512: tt * 2048 + (d4 + 1) * 512]
                g = onescol[:] if j < 2 else g_sb[:, tt * 16 + (j - 2): tt * 16 + (j - 1)]
                pb_rel[bi] = p.emit(
                    "vector", lambda e, o=av, i=bank, gc=g:
                    e.scalar_tensor_tensor(out=o, in0=i[:], scalar=gc, in1=o,
                                           op0=ALU.mult, op1=ALU.add), s_v)
        h_rel = s_mm.count
        w2_rel = s_mm.count
        b2_rel[0] = s_mm.count

    emit_residual_out(p, lb, acc, x_dram, xo_dram, coef, s_x, s_v, s_out)


def emit_attn0_qkv(p, nc, x_dram, wq, wk, wv, cosd, sind, qout, kfm, vout,
                   ident_sb, tag):
    s_x = p.sem(); s_v = p.sem(); s_sc = p.sem()
    s_mm = p.sem(); s_out = p.sem()
    w_s = Stream(p, 2)
    tp = TP(p, ident_sb)
    lb = LNBufs(nc, tag)
    u_tok = nc.alloc_sbuf_tensor(f"utok{tag}", [128, 2 * 2048], BF16)
    u_fm = nc.alloc_sbuf_tensor(f"ufm{tag}", [128, 16 * 1024], BF16)
    w_sb = nc.alloc_sbuf_tensor(f"wsb{tag}", [128, 2 * 16 * 512], BF16)
    evt = nc.alloc_sbuf_tensor(f"evt{tag}", [128, 2 * 512], BF16)
    rot = nc.alloc_sbuf_tensor(f"rot{tag}", [128, 2 * 512], BF16)
    t1 = nc.alloc_sbuf_tensor(f"t1{tag}", [128, 256], F32)
    t2 = nc.alloc_sbuf_tensor(f"t2{tag}", [128, 256], F32)
    cos_sb = nc.alloc_sbuf_tensor(f"cos{tag}", [128, 8 * 128], F32)
    sin_sb = nc.alloc_sbuf_tensor(f"sin{tag}", [128, 8 * 128], F32)
    k_fm_sb = nc.alloc_sbuf_tensor(f"kfm{tag}", [128, 8 * 1024], BF16)
    pq = [nc.alloc_psum_tensor(f"pq{tag}{i}", [128, 512], F32) for i in range(2)]
    pq_rel = [0, 0]

    p.dma("sync", cos_sb[:].rearrange("p (t j) -> p t j", t=8),
          cosd.rearrange("(t p) j -> p t j", p=128), s_x)
    p.dma("sync", sin_sb[:].rearrange("p (t j) -> p t j", t=8),
          sind.rearrange("(t p) j -> p t j", p=128), s_x)
    tab_ready = s_x.count

    emit_ln(p, lb, x_dram, s_x, s_v, s_sc, u_tok, tp=tp, u_fm=u_fm, u_tiles=2)
    u_fm_ready = tp.s_ev.count

    jobs = [(wq, qout, 8, True, False), (wk, None, 2, True, True),
            (wv, vout, 1, False, False)]
    w_rel = {0: 0, 1: 0}
    ev_rel = {0: 0, 1: 0}
    wbuf_i = 0
    p.wait("tensor", tp.s_ev, u_fm_ready)
    for (wap, outap, n8, do_rope, do_fm) in jobs:
        for of8 in range(n8):
            wb = wbuf_i % 2
            wbuf_i += 1
            src = wap[:, of8 * 512:(of8 + 1) * 512].rearrange(
                "(dc p) f -> p dc f", p=128)
            dst = w_sb[:, wb * 8192:(wb + 1) * 8192].rearrange(
                "p (dc f) -> p dc f", dc=16)
            w_loaded = w_s.dma("gpsimd", dst, src, gate=(s_mm, w_rel[wb]), slot=wb)
            p.wait("tensor", w_loaded[0], w_loaded[1])
            for tt in range(NTILE):
                bank = pq[tt % 2]
                p.wait("tensor", s_v, pq_rel[tt % 2])
                for dc in range(NDC):
                    lhs = u_fm[:, dc * 1024 + tt * 128: dc * 1024 + (tt + 1) * 128]
                    rhs = w_sb[:, wb * 8192 + dc * 512: wb * 8192 + (dc + 1) * 512]
                    p.emit("tensor", lambda e, o=bank, l=lhs, r=rhs, s=(dc == 0),
                           t2_=(dc == 15): e.matmul(o[:], l, r, start=s, stop=t2_),
                           s_mm if dc == 15 else None)
                if tt == NTILE - 1:
                    w_rel[wb] = s_mm.count
                mm_done = s_mm.count
                eb = tt % 2
                ev = evt[:, eb * 512:(eb + 1) * 512]
                ro = rot[:, eb * 512:(eb + 1) * 512]
                p.wait("vector", s_mm, mm_done)
                p.wait("vector", s_out, ev_rel[eb])
                pq_rel[tt % 2] = p.emit("vector", lambda e, o=ev, i=bank:
                                        e.tensor_copy(o, i[:]), s_v)
                if do_rope:
                    p.wait("vector", s_x, tab_ready)
                    for hh in range(2):
                        base = hh * 256
                        re = ev[:, base:base + 256].rearrange("p (j two) -> p two j", two=2)[:, 0]
                        im = ev[:, base:base + 256].rearrange("p (j two) -> p two j", two=2)[:, 1]
                        reo = ro[:, base:base + 256].rearrange("p (j two) -> p two j", two=2)[:, 0]
                        imo = ro[:, base:base + 256].rearrange("p (j two) -> p two j", two=2)[:, 1]
                        cs = cos_sb[:, tt * 128:(tt + 1) * 128]
                        sn = sin_sb[:, tt * 128:(tt + 1) * 128]
                        p.emit("vector", lambda e, o=t1, a=re, b=cs:
                               e.tensor_tensor(out=o[:, 0:128], in0=a, in1=b, op=ALU.mult))
                        p.emit("vector", lambda e, o=t2, a=im, b=sn:
                               e.tensor_tensor(out=o[:, 0:128], in0=a, in1=b, op=ALU.mult), s_v)
                        p.wait("vector", s_v)
                        p.emit("vector", lambda e, o=reo, a=t1, b=t2:
                               e.tensor_tensor(out=o, in0=a[:, 0:128], in1=b[:, 0:128],
                                               op=ALU.subtract))
                        p.emit("vector", lambda e, o=t1, a=re, b=sn:
                               e.tensor_tensor(out=o[:, 0:128], in0=a, in1=b, op=ALU.mult))
                        p.emit("vector", lambda e, o=t2, a=im, b=cs:
                               e.tensor_tensor(out=o[:, 0:128], in0=a, in1=b, op=ALU.mult), s_v)
                        p.wait("vector", s_v)
                        p.emit("vector", lambda e, o=imo, a=t1, b=t2:
                               e.tensor_tensor(out=o, in0=a[:, 0:128], in1=b[:, 0:128],
                                               op=ALU.add), s_v)
                    outsrc = ro
                else:
                    outsrc = ev
                fin = s_v.count
                if do_fm:
                    for j4 in range(2):
                        srcs = [outsrc[:, (j4 * 2 + jj) * 128:(j4 * 2 + jj + 1) * 128]
                                for jj in range(2)]
                        dsts = [k_fm_sb[:, (of8 * 4 + j4 * 2 + jj) * 1024 + tt * 128:
                                        (of8 * 4 + j4 * 2 + jj) * 1024 + (tt + 1) * 128]
                                for jj in range(2)]
                        tp.group(srcs, dsts, src_ready=(s_v, fin))
                    ev_rel[eb] = tp.s_tp.count * 0 + s_out.count
                    ev_rel[eb] = 0
                    p.wait("vector", tp.s_tp, tp.s_tp.count)
                else:
                    p.wait("scalar", s_v, fin)
                    p.dma("scalar", outap[tt * 128:(tt + 1) * 128,
                                          of8 * 512:(of8 + 1) * 512], outsrc, s_out)
                    ev_rel[eb] = s_out.count
    kfm_ready = tp.s_ev.count
    p.wait("scalar", tp.s_ev, kfm_ready)
    for fc in range(8):
        p.dma("scalar", kfm[fc * 128:(fc + 1) * 128, :],
              k_fm_sb[:, fc * 1024:(fc + 1) * 1024], s_out)
    p.wait("scalar", s_out, s_out.count)


def emit_attn0_heads(p, nc, q_dram, kg, vg, lq1, lq2, lk1, lk2, att_out,
                     masks_d, ident_sb, ident32_sb, tag):
    s_x = p.sem(); s_v = p.sem(); s_sc = p.sem(); s_k = p.sem()
    s_mm = p.sem(); s_out = p.sem(); s_c = p.sem()
    tp = TP(p, ident_sb)
    k_sb = nc.alloc_sbuf_tensor(f"k{tag}", [128, 2 * 8192], BF16)
    v_sb = nc.alloc_sbuf_tensor(f"v{tag}", [128, 8192], BF16)
    p0_sb = nc.alloc_sbuf_tensor(f"p0{tag}", [128, 8192], BF16)
    p1_sb = nc.alloc_sbuf_tensor(f"p1{tag}", [128, 8192], BF16)
    att_sb = nc.alloc_sbuf_tensor(f"att{tag}", [128, 2 * 4 * 256], BF16)
    qst = nc.alloc_sbuf_tensor(f"qst{tag}", [128, 2 * 4 * 256], BF16)
    q_fm = nc.alloc_sbuf_tensor(f"qfm{tag}", [128, 2 * 2 * 4 * 128], BF16)
    msk_sb = nc.alloc_sbuf_tensor(f"msk{tag}", [128, 4 * 512], F32)
    wT = nc.alloc_sbuf_tensor(f"wT{tag}", [128, 2 * 128], BF16)
    zb = nc.alloc_sbuf_tensor(f"zb{tag}", [128, 2 * 8], F32)
    Z0 = nc.alloc_sbuf_tensor(f"Z0{tag}", [128, 1], F32)
    Z1 = nc.alloc_sbuf_tensor(f"Z1{tag}", [128, 1], F32)
    rz0 = nc.alloc_sbuf_tensor(f"rz0{tag}", [128, 1], F32)
    rz1 = nc.alloc_sbuf_tensor(f"rz1{tag}", [128, 1], F32)
    rz2 = nc.alloc_sbuf_tensor(f"rz2{tag}", [128, 1], F32)
    Z2 = nc.alloc_sbuf_tensor(f"Z2{tag}", [128, 1], F32)
    c1n = nc.alloc_sbuf_tensor(f"c1n{tag}", [128, 1], F32)
    lamv = nc.alloc_sbuf_tensor(f"lam{tag}", [2, 260], F32)
    lam_b = nc.alloc_sbuf_tensor(f"lamb{tag}", [128, 2], F32)
    lamrow = nc.alloc_sbuf_tensor(f"lamr{tag}", [1, 2], F32)
    ones1f = nc.alloc_sbuf_tensor(f"onf{tag}", [1, 128], F32)
    ssb = nc.alloc_sbuf_tensor(f"ss{tag}", [128, 4 * 2], F32)
    rzb = nc.alloc_sbuf_tensor(f"rzb{tag}", [128, 4 * 2], F32)
    hsc = nc.alloc_sbuf_tensor(f"hsc{tag}", [128, 2], F32)
    junk = nc.alloc_sbuf_tensor(f"jk{tag}", [128, 128], BF16)
    lqk = nc.alloc_sbuf_tensor(f"lqk{tag}", [2, 4 * 128], F32)
    bm20 = nc.alloc_sbuf_tensor(f"bm20{tag}", [128, 1], F32)
    bm2 = nc.alloc_sbuf_tensor(f"bm2{tag}", [128, 1], F32)

    pe = [nc.alloc_psum_tensor(f"pe{tag}{i}", [128, 1024], F32) for i in range(2)]
    patt = nc.alloc_psum_tensor(f"patt{tag}", [128, 512], F32)

    p.emit("vector", lambda e: e.memset(ones1f[:], 1.0), s_v)
    p.emit("vector", lambda e: e.memset(bm20[:], -20.0), s_v)
    p.emit("vector", lambda e: e.memset(bm2[:], -2.0), s_v)
    bias_ready = s_v.count
    p.dma("sync", msk_sb[:], masks_d[:], s_c)
    for i, ap in enumerate([lq1, lq2, lk1, lk2]):
        p.dma("sync", lqk[0:2, i * 128:(i + 1) * 128], ap[:], s_c)
    const_in = s_c.count
    p.wait("scalar", s_v, bias_ready)
    p.dma("gpsimd", k_sb[:, 0:8192], kg[0:128, :], s_k)
    p.dma("gpsimd", k_sb[:, 8192:16384], kg[128:256, :], s_k)
    p.dma("gpsimd", v_sb[:].rearrange("p (b f) -> p b f", f=128),
          vg[:].rearrange("(b p) f -> p b f", p=128), s_k)
    kv_loaded = s_k.count
    p.wait("vector", s_c, const_in)
    p.emit("vector", lambda e: e.tensor_tensor(
        out=lamv[0:2, 0:128], in0=lqk[0:2, 0:128], in1=lqk[0:2, 256:384],
        op=ALU.mult), s_v)
    p.emit("vector", lambda e: e.tensor_tensor(
        out=lamv[0:2, 128:256], in0=lqk[0:2, 128:256], in1=lqk[0:2, 384:512],
        op=ALU.mult), s_v)
    p.wait("vector", s_v)
    p.emit("vector", lambda e: e.reduce_sum(lamv[0:2, 256:257], lamv[0:2, 0:128],
                                            axis=AX.X), s_v)
    p.emit("vector", lambda e: e.reduce_sum(lamv[0:2, 257:258], lamv[0:2, 128:256],
                                            axis=AX.X), s_v)
    p.wait("scalar", s_v, s_v.count)
    p.emit("scalar", lambda e: e.activation(out=lamv[0:2, 256:258],
                                            in_=lamv[0:2, 256:258],
                                            func=AF.Exp), s_sc)
    p.wait("vector", s_sc, s_sc.count)
    p.emit("vector", lambda e: e.tensor_tensor(out=lamv[0:2, 258:259],
                                               in0=lamv[0:2, 256:257],
                                               in1=lamv[0:2, 257:258],
                                               op=ALU.subtract), s_v)
    p.wait("vector", s_v)
    p.emit("vector", lambda e: e.tensor_scalar(out=lamv[0:2, 258:259],
                                               in0=lamv[0:2, 258:259], scalar1=0.2,
                                               scalar2=None, op0=ALU.add,
                                               op1=ALU.bypass), s_v)
    lam_done = s_v.count
    p.wait("tensor", s_v, lam_done)
    p.emit("tensor", lambda e: e.transpose(patt[0:1, 0:2], lamv[0:2, 258:259],
                                           ident32_sb[0:2, 0:2]), s_mm)
    p.wait("vector", s_mm, s_mm.count)
    p.emit("vector", lambda e: e.tensor_copy(lamrow[:], patt[0:1, 0:2]), s_v)
    p.wait("tensor", s_v, s_v.count)
    p.emit("tensor", lambda e: e.matmul(patt[0:128, 0:2], ones1f[:], lamrow[:],
                                        start=True, stop=True), s_mm)
    p.wait("vector", s_mm, s_mm.count)
    p.emit("vector", lambda e: e.tensor_copy(lam_b[:], patt[0:128, 0:2]), s_v)
    lam_ready = s_v.count

    SC = 1.0 / np.sqrt(128.0)
    p0_rel = 0
    p1_rel = 0
    pe_rel = [0, 0]
    att_rel = 0
    wT_rel = [0, 0]
    qst_rel = 0
    out_rel = {}
    out_all_prev = 0

    for s in range(16):
        L = (s + 1) * 512
        ng = s + 1
        for hl in range(2):
            p.wait("sync", tp.s_tp, qst_rel)
            p.wait("sync", s_out, out_rel.get(hl, 0))
            for qt in range(4):
                p.dma("sync", qst[:, (hl * 4 + qt) * 256:(hl * 4 + qt + 1) * 256],
                      q_dram[s * 512 + qt * 128: s * 512 + (qt + 1) * 128,
                             hl * 256:(hl + 1) * 256], s_x)
            q_loaded = s_x.count
            for qt in range(4):
                for c in range(2):
                    tp.group([qst[:, (hl * 4 + qt) * 256 + c * 128:
                                  (hl * 4 + qt) * 256 + (c + 1) * 128]],
                             [q_fm[:, ((hl * 2 + c) * 4 + qt) * 128:
                                   ((hl * 2 + c) * 4 + qt + 1) * 128]],
                             src_ready=(s_x, q_loaded))
            qst_rel = tp.s_tp.count
            qfm_ready = tp.s_ev.count
            for qt in range(4):
                NB = s * 4 + qt + 1
                for c in range(2):
                    pz = p0_sb if c == 0 else p1_sb
                    zv = zb[:, c * 8:(c + 1) * 8]
                    for gp in range(0, ng, 2):
                        take = min(2, ng - gp)
                        bank = pe[(gp // 2) % 2]
                        p.wait("tensor", s_sc, pe_rel[(gp // 2) % 2])
                        if gp == 0:
                            p.wait("tensor", tp.s_ev, qfm_ready)
                            p.wait("tensor", s_k, kv_loaded)
                        for g2 in range(take):
                            g = gp + g2
                            lhs = q_fm[:, ((hl * 2 + c) * 4 + qt) * 128:
                                       ((hl * 2 + c) * 4 + qt + 1) * 128]
                            rhs = k_sb[:, c * 8192 + g * 512: c * 8192 + (g + 1) * 512]
                            p.emit("tensor", lambda e, o=bank[:, g2 * 512:(g2 + 1) * 512],
                                   l=lhs, r=rhs: e.matmul(o, l, r, start=True,
                                                          stop=True), s_mm)
                        mm_done = s_mm.count
                        if gp <= s < gp + take:
                            g2 = s - gp
                            p.wait("vector", s_mm, mm_done)
                            p.emit("vector", lambda e, o=bank[:, g2 * 512:(g2 + 1) * 512],
                                   m=msk_sb[:, qt * 512:(qt + 1) * 512]:
                                   e.tensor_tensor(out=o, in0=o, in1=m, op=ALU.add), s_v)
                            p.wait("scalar", s_v, s_v.count)
                        p.wait("scalar", s_mm, mm_done)
                        if c == 1:
                            p.wait("scalar", tp.s_tp, p1_rel)
                        else:
                            p.wait("scalar", s_v, p0_rel)
                        pe_rel[(gp // 2) % 2] = p.emit(
                            "scalar", lambda e, o=pz[:, gp * 512: gp * 512 + take * 512],
                            i=bank[:, 0:take * 512], z=zv[:, gp // 2: gp // 2 + 1]:
                            e.activation(out=o, in_=i, func=AF.Exp, bias=bm20[:],
                                         scale=SC, accum_out=z), s_sc)
                    p.wait("vector", s_sc, s_sc.count)
                    Zc = Z0 if c == 0 else Z1
                    p.emit("vector", lambda e, o=Zc, i=zv[:, 0:(ng + 1) // 2]:
                           e.reduce_sum(o[:], i, axis=AX.X), s_v)
                    p.wait("vector", s_v)
                    rc = rz0 if c == 0 else rz1
                    p.emit("vector", lambda e, o=rc, i=Zc: e.reciprocal(o[:], i[:]), s_v)
                    p.wait("vector", s_v)
                p.wait("vector", s_v, lam_ready)
                p.emit("vector", lambda e, o=c1n, a=Z0, b=rz1: e.tensor_tensor(
                    out=o[:], in0=a[:], in1=b[:], op=ALU.mult), s_v)
                p.wait("vector", s_v)
                p.emit("vector", lambda e, o=c1n, i=lam_b[:, hl:hl + 1]: e.tensor_tensor(
                    out=o[:], in0=o[:], in1=i, op=ALU.mult), s_v)
                p.wait("vector", s_v)
                p.emit("vector", lambda e, o=c1n: e.tensor_scalar(
                    out=o[:], in0=o[:], scalar1=-1.0, scalar2=None, op0=ALU.mult,
                    op1=ALU.bypass), s_v)
                p.wait("vector", s_v)
                p.emit("vector", lambda e, o=p1_sb[:, 0:L], a=p1_sb[:, 0:L],
                       b=p0_sb[:, 0:L], c1=c1n: e.scalar_tensor_tensor(
                           out=o, in0=a, scalar=c1[:], in1=b, op0=ALU.mult,
                           op1=ALU.add), s_v)
                p0_rel = s_v.count
                p.wait("vector", s_v)
                p.emit("vector", lambda e, o=p1_sb[:, s * 512:(s + 1) * 512],
                       m=msk_sb[:, qt * 512:(qt + 1) * 512]:
                       e.tensor_tensor(out=o, in0=o, in1=m, op=ALU.add), s_v)
                w_done = s_v.count
                p.wait("scalar", s_v, w_done)
                exp2_done = p.emit("scalar", lambda e, o=p1_sb[:, 0:L],
                                   i=p1_sb[:, 0:L], sc2=rz0, z=Z2[:]:
                                   e.activation(out=o, in_=i, func=AF.Exp, bias=bm2[:],
                                                scale=sc2[:], accum_out=z), s_sc)
                p.wait("vector", s_sc, exp2_done)
                p.emit("vector", lambda e, o=rz2, i=Z2: e.reciprocal(o[:], i[:]), s_v)
                rz2_done = s_v.count
                first_tp = True
                for kb in range(NB):
                    slot = kb % 2
                    b = tp.i % len(tp.banks)
                    tp.i += 1
                    p.wait("tensor", tp.s_ev, tp.rel[b])
                    if first_tp:
                        p.wait("tensor", s_sc, exp2_done)
                        first_tp = False
                    bank = tp.banks[b]
                    p.emit("tensor", lambda e, o=bank[:, 0:128],
                           i=p1_sb[:, kb * 128:(kb + 1) * 128], idn=ident_sb:
                           e.transpose(o, i, idn[:]), tp.s_tp)
                    p.wait("vector", tp.s_tp, tp.s_tp.count)
                    p.wait("vector", s_mm, wT_rel[slot])
                    p.emit("vector", lambda e, o=wT[:, slot * 128:(slot + 1) * 128],
                           i=bank[:, 0:128]: e.tensor_copy(o, i), tp.s_ev)
                    tp.rel[b] = tp.s_ev.count
                    ev_done = tp.s_ev.count
                    p.wait("tensor", tp.s_ev, ev_done)
                    if kb == 0:
                        p.wait("tensor", s_v, att_rel)
                    wT_rel[slot] = p.emit(
                        "tensor", lambda e, o=patt[:, 0:128],
                        l=wT[:, slot * 128:(slot + 1) * 128],
                        r=v_sb[:, kb * 128:(kb + 1) * 128], st=(kb == 0),
                        sp=(kb == NB - 1): e.matmul(o, l, r, start=st, stop=sp),
                        s_mm)
                p1_rel = tp.s_tp.count
                av_done = s_mm.count
                p.wait("scalar", s_mm, av_done)
                sq_done = p.emit("scalar", lambda e, o=junk, i=patt[:, 0:128],
                                 z=ssb[:, qt * 2 + hl: qt * 2 + hl + 1]:
                                 e.activation(out=o[:], in_=i, func=AF.Square,
                                              accum_out=z), s_sc)
                p.wait("vector", s_sc, sq_done)
                p.wait("vector", s_v, rz2_done)
                p.wait("vector", s_out, out_all_prev)
                p.emit("vector", lambda e, o=rzb[:, qt * 2 + hl: qt * 2 + hl + 1],
                       i=rz2: e.tensor_copy(o, i[:]), s_v)
                att_rel = p.emit("vector", lambda e,
                                 o=att_sb[:, (hl * 4 + qt) * 256:
                                          (hl * 4 + qt) * 256 + 128],
                                 i=patt[:, 0:128], r=rz2:
                                 e.tensor_scalar(out=o, in0=i, scalar1=r[:],
                                                 scalar2=None, op0=ALU.mult,
                                                 op1=ALU.bypass), s_v)
        for qt in range(4):
            p.emit("vector", lambda e, o=hsc, a=ssb[:, qt * 2:(qt + 1) * 2],
                   b=rzb[:, qt * 2:(qt + 1) * 2]:
                   e.tensor_tensor(out=o[:], in0=a, in1=b, op=ALU.mult), s_v)
            p.wait("vector", s_v)
            p.emit("vector", lambda e, o=hsc, b=rzb[:, qt * 2:(qt + 1) * 2]:
                   e.tensor_tensor(out=o[:], in0=o[:], in1=b, op=ALU.mult), s_v)
            p.wait("vector", s_v)
            p.emit("vector", lambda e, o=hsc: e.tensor_scalar(
                out=o[:], in0=o[:], scalar1=1.0 / 128.0, scalar2=1e-8,
                op0=ALU.mult, op1=ALU.add), s_v)
            p.wait("scalar", s_v, s_v.count)
            p.emit("scalar", lambda e, o=hsc: e.activation(out=o[:], in_=o[:],
                                                           func=AF.Ln), s_sc)
            p.wait("scalar", s_sc, s_sc.count)
            ex_done = p.emit("scalar", lambda e, o=hsc: e.activation(
                out=o[:], in_=o[:], func=AF.Exp, scale=-0.5), s_sc)
            p.wait("vector", s_sc, ex_done)
            for hl in range(2):
                p.emit("vector", lambda e,
                       o=att_sb[:, (hl * 4 + qt) * 256:(hl * 4 + qt) * 256 + 128],
                       sc2=hsc[:, hl:hl + 1]:
                       e.tensor_scalar(out=o, in0=o, scalar1=sc2, scalar2=None,
                                       op0=ALU.mult, op1=ALU.bypass), s_v)
        att_scaled = s_v.count
        p.wait("scalar", s_v, att_scaled)
        for hl in range(2):
            for qt in range(4):
                p.dma("scalar",
                      att_out[s * 512 + qt * 128: s * 512 + (qt + 1) * 128,
                              hl * 128:(hl + 1) * 128],
                      att_sb[:, (hl * 4 + qt) * 256:(hl * 4 + qt) * 256 + 128],
                      s_out)
            out_rel[hl] = s_out.count
        out_all_prev = s_out.count
    p.wait("scalar", s_out, s_out.count)


def emit_attn0_out(p, nc, x_dram, xo_dram, att_dram, wo, rms_w, ident_sb, tag):
    s_x = p.sem(); s_v = p.sem(); s_sc = p.sem()
    s_mm = p.sem(); s_out = p.sem(); s_r = p.sem()
    at_s = Stream(p, 2)
    wo_s = Stream(p, 2)
    tp = TP(p, ident_sb)
    att_tok = nc.alloc_sbuf_tensor(f"at{tag}", [128, 2 * 2048], BF16)
    att_fm = nc.alloc_sbuf_tensor(f"afm{tag}", [128, 16 * 1024], BF16)
    wo_sb = nc.alloc_sbuf_tensor(f"wo{tag}", [128, 2 * 16 * 512], BF16)
    x_sb = nc.alloc_sbuf_tensor(f"xs{tag}", [128, 8 * 2048], F32)
    rmsw = nc.alloc_sbuf_tensor(f"rw{tag}", [128, 1], F32)
    po = [nc.alloc_psum_tensor(f"po{tag}{i}", [128, 512], F32) for i in range(2)]
    po_rel = [0, 0]

    p.dma("sync", rmsw[:], rms_w[:, None], s_r)
    p.wait("vector", s_r, s_r.count)
    p.emit("vector", lambda e: e.tensor_scalar(out=rmsw[:], in0=rmsw[:], scalar1=0.8,
                                               scalar2=None, op0=ALU.mult,
                                               op1=ALU.bypass), s_v)
    rmsw_ready = s_v.count

    at_rel = {0: 0, 1: 0}
    for t in range(NTILE):
        ab = t % 2
        av = att_tok[:, ab * 2048:(ab + 1) * 2048]
        a_loaded = at_s.dma("sync", av, att_dram[t * 128:(t + 1) * 128, :],
                            gate=(tp.s_tp, at_rel[ab]), slot=ab)
        for g in range(4):
            srcs = [av[:, (g * 4 + j) * 128:(g * 4 + j + 1) * 128] for j in range(4)]
            dsts = [att_fm[:, (g * 4 + j) * 1024 + t * 128:
                           (g * 4 + j) * 1024 + (t + 1) * 128] for j in range(4)]
            tp.group(srcs, dsts, src_ready=a_loaded)
        at_rel[ab] = tp.s_tp.count
    attfm_ready = tp.s_ev.count

    for t in range(NTILE):
        p.dma("sync", x_sb[:, t * 2048:(t + 1) * 2048],
              x_dram[t * 128:(t + 1) * 128, :], s_x)
    x_all = s_x.count

    wo_rel = {0: 0, 1: 0}
    for oc in range(4):
        wb = oc % 2
        wo_loaded = wo_s.dma(
            "gpsimd", wo_sb[:, wb * 8192:(wb + 1) * 8192].rearrange(
                "p (ic f) -> p ic f", ic=16),
            wo[:, oc * 512:(oc + 1) * 512].rearrange("(ic p) f -> p ic f", p=128),
            gate=(s_mm, wo_rel[wb]), slot=wb)
        p.wait("vector", wo_loaded[0], wo_loaded[1])
        p.wait("vector", s_v, rmsw_ready)
        wo_scaled = p.emit("vector", lambda e, o=wo_sb[:, wb * 8192:(wb + 1) * 8192],
                           r=rmsw: e.tensor_scalar(out=o, in0=o, scalar1=r[:],
                                                   scalar2=None, op0=ALU.mult,
                                                   op1=ALU.bypass), s_v)
        for t in range(NTILE):
            bank = po[t % 2]
            p.wait("tensor", s_v, po_rel[t % 2])
            p.wait("tensor", s_v, wo_scaled)
            if oc == 0 and t == 0:
                p.wait("tensor", tp.s_ev, attfm_ready)
            for ic in range(16):
                lhs = att_fm[:, ic * 1024 + t * 128: ic * 1024 + (t + 1) * 128]
                rhs = wo_sb[:, wb * 8192 + ic * 512: wb * 8192 + (ic + 1) * 512]
                p.emit("tensor", lambda e, o=bank, l=lhs, r=rhs,
                       st=(ic == 0), sp=(ic == 15):
                       e.matmul(o[:], l, r, start=st, stop=sp),
                       s_mm if ic == 15 else None)
            if t == NTILE - 1:
                wo_rel[wb] = s_mm.count
            prj_done = s_mm.count
            xv = x_sb[:, t * 2048 + oc * 512: t * 2048 + (oc + 1) * 512]
            p.wait("vector", s_x, x_all)
            p.wait("vector", s_mm, prj_done)
            po_rel[t % 2] = p.emit("vector", lambda e, o=xv, i=bank:
                                   e.tensor_tensor(out=o, in0=o, in1=i[:],
                                                   op=ALU.add), s_v)
            if oc == 3:
                p.wait("scalar", s_v, s_v.count)
                p.dma("scalar", xo_dram[t * 128:(t + 1) * 128, :],
                      x_sb[:, t * 2048:(t + 1) * 2048], s_out)
    p.wait("scalar", s_out, s_out.count)


def emit_attn1_qkv(p, nc, x_dram, wqkv, wg1, wg2, ld_dram, qo, ko, vo, go, mout,
                   colv4_d, ident_sb, tag):
    s_x = p.sem(); s_v = p.sem(); s_sc = p.sem()
    s_mm = p.sem(); s_out = p.sem(); s_gw = p.sem()
    w_s = Stream(p, 2)
    tp = TP(p, ident_sb)
    lb = LNBufs(nc, tag)
    u_tok = nc.alloc_sbuf_tensor(f"utok{tag}", [128, 2 * 2048], BF16)
    u_fm = nc.alloc_sbuf_tensor(f"ufm{tag}", [128, 16 * 1024], BF16)
    w_sb = nc.alloc_sbuf_tensor(f"wsb{tag}", [128, 2 * 16 * 512], BF16)
    k_sb = nc.alloc_sbuf_tensor(f"ksb{tag}", [128, 8 * 2048], BF16)
    v_sb = nc.alloc_sbuf_tensor(f"vsb{tag}", [128, 8 * 2048], BF16)
    ev_sb = nc.alloc_sbuf_tensor(f"ev{tag}", [128, 2 * 512], BF16)
    g1fm = nc.alloc_sbuf_tensor(f"g1f{tag}", [128, 8 * 128], BF16)
    wg1_sb = nc.alloc_sbuf_tensor(f"wg1{tag}", [128, 16 * 128], BF16)
    wg2_sb = nc.alloc_sbuf_tensor(f"wg2{tag}", [128, 2048], BF16)
    colv4 = nc.alloc_sbuf_tensor(f"cv4{tag}", [1, 4 * 128], F32)
    ldrow = nc.alloc_sbuf_tensor(f"ldr{tag}", [1, 16], F32)
    kdecS = nc.alloc_sbuf_tensor(f"kdS{tag}", [128, 4 * 16], F32)
    kdk = nc.alloc_sbuf_tensor(f"kdk{tag}", [128, 128], BF16)
    mst = nc.alloc_sbuf_tensor(f"mst{tag}", [128, 128], F32)
    pq = [nc.alloc_psum_tensor(f"pq{tag}{i}", [128, 512], F32) for i in range(2)]
    pm = nc.alloc_psum_tensor(f"pm{tag}", [128, 128], F32)
    pq_rel = [0, 0]

    p.dma("sync", colv4[:], colv4_d[:], s_x)
    p.dma("sync", ldrow[:], ld_dram[None, :], s_x)
    cready = s_x.count
    p.wait("tensor", s_x, cready)
    for c in range(4):
        p.emit("tensor", lambda e, o=pm[0:128, 0:16],
               l=colv4[:, c * 128:(c + 1) * 128], r=ldrow:
               e.matmul(o, l, r[:], start=True, stop=True), s_mm)
        p.wait("scalar", s_mm, s_mm.count)
        p.emit("scalar", lambda e, o=kdecS[:, c * 16:(c + 1) * 16], i=pm[0:128, 0:16]:
               e.activation(out=o, in_=i, func=AF.Exp), s_sc)
        p.wait("tensor", s_sc, s_sc.count)

    emit_ln(p, lb, x_dram, s_x, s_v, s_sc, u_tok, tp=tp, u_fm=u_fm, u_tiles=2)
    u_fm_ready = tp.s_ev.count
    p.wait("tensor", tp.s_ev, u_fm_ready)

    w_rel = {0: 0, 1: 0}
    ev_rel = {0: 0, 1: 0}
    wbuf_i = 0
    for wi, (wap, outap) in enumerate(zip(wqkv, [qo, ko, vo])):
        for of4 in range(4):
            wb = wbuf_i % 2
            wbuf_i += 1
            src = wap[:, of4 * 512:(of4 + 1) * 512].rearrange(
                "(dc p) f -> p dc f", p=128)
            dst = w_sb[:, wb * 8192:(wb + 1) * 8192].rearrange(
                "p (dc f) -> p dc f", dc=16)
            w_loaded = w_s.dma("gpsimd", dst, src, gate=(s_mm, w_rel[wb]), slot=wb)
            p.wait("tensor", w_loaded[0], w_loaded[1])
            for tt in range(NTILE):
                bank = pq[tt % 2]
                p.wait("tensor", s_sc, pq_rel[tt % 2])
                for dc in range(NDC):
                    lhs = u_fm[:, dc * 1024 + tt * 128: dc * 1024 + (tt + 1) * 128]
                    rhs = w_sb[:, wb * 8192 + dc * 512: wb * 8192 + (dc + 1) * 512]
                    p.emit("tensor", lambda e, o=bank, l=lhs, r=rhs, s=(dc == 0),
                           t2_=(dc == 15): e.matmul(o[:], l, r, start=s, stop=t2_),
                           s_mm if dc == 15 else None)
                if tt == NTILE - 1:
                    w_rel[wb] = s_mm.count
                mm_done = s_mm.count
                p.wait("scalar", s_mm, mm_done)
                if wi == 1:
                    dstv = k_sb[:, tt * 2048 + of4 * 512: tt * 2048 + (of4 + 1) * 512]
                elif wi == 2:
                    dstv = v_sb[:, tt * 2048 + of4 * 512: tt * 2048 + (of4 + 1) * 512]
                else:
                    dstv = ev_sb[:, (tt % 2) * 512:(tt % 2 + 1) * 512]
                    p.wait("scalar", s_out, ev_rel[tt % 2])
                pq_rel[tt % 2] = p.emit("scalar", lambda e, o=dstv, i=bank:
                                        e.activation(out=o, in_=i[:], func=AF.Silu),
                                        s_sc)
                p.wait("scalar", s_sc, s_sc.count)
                p.dma("scalar", outap[tt * 128:(tt + 1) * 128,
                                      of4 * 512:(of4 + 1) * 512], dstv, s_out)
                if wi == 0:
                    ev_rel[tt % 2] = s_out.count
    qkv_done = s_sc.count

    p.dma("gpsimd", wg1_sb[:].rearrange("p (dc f) -> p dc f", dc=16),
          wg1.rearrange("(dc p) f -> p dc f", p=128), s_gw)
    p.dma("gpsimd", wg2_sb[:], wg2[:], s_gw)
    g_loaded = s_gw.count
    p.wait("tensor", s_gw, g_loaded)
    for tt in range(NTILE):
        bank = pq[tt % 2]
        p.wait("tensor", s_sc, pq_rel[tt % 2])
        p.wait("tensor", s_v, 0)
        for dc in range(NDC):
            lhs = u_fm[:, dc * 1024 + tt * 128: dc * 1024 + (tt + 1) * 128]
            rhs = wg1_sb[:, dc * 128:(dc + 1) * 128]
            p.emit("tensor", lambda e, o=bank[:, 0:128], l=lhs, r=rhs, s=(dc == 0),
                   t2_=(dc == 15): e.matmul(o, l, r, start=s, stop=t2_),
                   s_mm if dc == 15 else None)
        g1_done = s_mm.count
        p.wait("scalar", s_mm, g1_done)
        eb = tt % 2
        p.wait("scalar", s_out, ev_rel[eb])
        ev = ev_sb[:, eb * 512: eb * 512 + 128]
        pq_rel[tt % 2] = p.emit("scalar", lambda e, o=ev, i=bank[:, 0:128]:
                                e.activation(out=o, in_=i, func=AF.Copy), s_sc)
        g1ev = s_sc.count
        tp.group([ev], [g1fm[:, tt * 128:(tt + 1) * 128]], src_ready=(s_sc, g1ev))
        ev_rel[eb] = 0
    g1fm_ready = tp.s_ev.count
    p.wait("tensor", tp.s_ev, g1fm_ready)
    p.wait("scalar", tp.s_tp, tp.s_tp.count)
    for tt in range(NTILE):
        for oc in range(4):
            bank = pq[(tt * 4 + oc) % 2]
            p.wait("tensor", s_sc, pq_rel[(tt * 4 + oc) % 2])
            p.emit("tensor", lambda e, o=bank,
                   l=g1fm[:, tt * 128:(tt + 1) * 128],
                   r=wg2_sb[:, oc * 512:(oc + 1) * 512]:
                   e.matmul(o[:], l, r, start=True, stop=True), s_mm)
            mm_done = s_mm.count
            p.wait("scalar", s_mm, mm_done)
            eb = (tt * 4 + oc) % 2
            p.wait("scalar", s_out, ev_rel[eb])
            ev = ev_sb[:, eb * 512:(eb + 1) * 512]
            pq_rel[eb] = p.emit("scalar", lambda e, o=ev, i=bank:
                                e.activation(out=o, in_=i[:], func=AF.Copy), s_sc)
            p.wait("scalar", s_sc, s_sc.count)
            p.dma("scalar", go[tt * 128:(tt + 1) * 128, oc * 512:(oc + 1) * 512],
                  ev, s_out)
            ev_rel[eb] = s_out.count

    kdk_rel = 0
    m_rel = 0
    for sl in range(2):
        for h in range(16):
            for c in range(4):
                t = sl * 4 + c
                kv = k_sb[:, t * 2048 + h * 128: t * 2048 + (h + 1) * 128]
                vv = v_sb[:, t * 2048 + h * 128: t * 2048 + (h + 1) * 128]
                p.wait("vector", s_sc, qkv_done)
                p.wait("vector", s_mm, kdk_rel)
                kd_done = p.emit("vector", lambda e, o=kdk, a=kv,
                                 s2=kdecS[:, c * 16 + h: c * 16 + h + 1]:
                                 e.tensor_scalar(out=o[:], in0=a, scalar1=s2,
                                                 scalar2=None, op0=ALU.mult,
                                                 op1=ALU.bypass), s_v)
                p.wait("tensor", s_v, kd_done)
                if c == 0:
                    p.wait("tensor", s_v, m_rel)
                kdk_rel = p.emit("tensor", lambda e, o=pm[:, 0:128], l=kdk[:], r=vv,
                                 st=(c == 0), sp=(c == 3):
                                 e.matmul(o, l, r, start=st, stop=sp), s_mm)
            p.wait("vector", s_mm, s_mm.count)
            m_rel = p.emit("vector", lambda e, o=mst, i=pm[:, 0:128]:
                           e.tensor_copy(o[:], i), s_v)
            p.wait("scalar", s_v, m_rel)
            p.dma("scalar", mout[sl, h], mst[:], s_out)
    p.wait("scalar", s_out, s_out.count)


def _finish_passthrough(p, nc, x_dram, xo_dram, xbuf, s_x, s_v, s_out, dep,
                        extra_mm=None):
    buf = nc.alloc_sbuf_tensor(f"ptx{p._semi}", [128, 2048], F32)
    p.wait("sync", dep[0], dep[1])
    if extra_mm is not None:
        p.wait("sync", extra_mm[0], extra_mm[1])
    prev = 0
    for t in range(NTILE):
        p.wait("sync", s_out, prev)
        p.dma("sync", buf[:], x_dram[t * 128:(t + 1) * 128, :], s_x)
        p.wait("scalar", s_x, s_x.count)
        p.dma("scalar", xo_dram[t * 128:(t + 1) * 128, :], buf[:], s_out)
        prev = s_out.count
    p.wait("scalar", s_out, s_out.count)


def emit_attn1_core(p, nc, x_dram, xo_dram, qd, kd_, vd, gd, m_all, ld_dram,
                    wo, norm_w, ecoef_d, dmat_d, ip1_d, ident_sb, ident32_sb, tag,
                    level=5, dbg=None):
    s_x = p.sem(); s_v = p.sem(); s_sc = p.sem(); s_w = p.sem()
    s_mm = p.sem(); s_out = p.sem(); s_m = p.sem(); s_g = p.sem()
    tp = TP(p, ident_sb)
    q_sb = nc.alloc_sbuf_tensor(f"q{tag}", [128, 8 * 2048], BF16)
    kv_sb = nc.alloc_sbuf_tensor(f"kv{tag}", [128, 16 * 2048], BF16)
    k_sb = kv_sb[:, 0:8 * 2048]
    v_sb = kv_sb[:, 8 * 2048:16 * 2048]
    o_sb = nc.alloc_sbuf_tensor(f"o{tag}", [128, 8 * 2048], BF16)
    _kv_addr = nc.lookup_mls(kv_sb).memorylocations[0].addr
    x_sb = nc.alloc_sbuf_tensor_at(f"xs{tag}", [128, 8 * 2048], F32,
                                   offset=_kv_addr)
    g_sb = nc.alloc_sbuf_tensor(f"g{tag}", [128, 2048], BF16)
    wo_sb = nc.alloc_sbuf_tensor(f"wo{tag}", [128, 2 * 16 * 512], BF16)
    Mh = nc.alloc_sbuf_tensor(f"Mh{tag}", [128, 16 * 128], F32)
    S_f = nc.alloc_sbuf_tensor(f"S{tag}", [128, 128], F32)
    S_bf = nc.alloc_sbuf_tensor(f"Sb{tag}", [128, 128], BF16)
    D_sb = nc.alloc_sbuf_tensor(f"D{tag}", [128, 16 * 128], F32)
    dmat = nc.alloc_sbuf_tensor(f"dm{tag}", [128, 128], F32)
    sc_T = nc.alloc_sbuf_tensor(f"scT{tag}", [128, 128], BF16)
    kdk = nc.alloc_sbuf_tensor(f"kdk{tag}", [128, 128], BF16)
    kqfm = nc.alloc_sbuf_tensor(f"kqf{tag}", [128, 3 * 128], BF16)
    ldrow = nc.alloc_sbuf_tensor(f"ldr{tag}", [1, 16], F32)
    ldb = nc.alloc_sbuf_tensor(f"ldb{tag}", [128, 16], F32)
    lamb = nc.alloc_sbuf_tensor(f"lmb{tag}", [128, 16], F32)
    qdec = nc.alloc_sbuf_tensor(f"qd{tag}", [128, 16], F32)
    kdec = nc.alloc_sbuf_tensor(f"kd{tag}", [128, 16], F32)
    ip1 = nc.alloc_sbuf_tensor(f"ip1{tag}", [1, 2 * 128], F32)
    ecoef = nc.alloc_sbuf_tensor(f"ec{tag}", [1, 32], F32)
    coefT = nc.alloc_sbuf_tensor(f"cT{tag}", [16, 16], F32)
    coefrow = nc.alloc_sbuf_tensor(f"cr{tag}", [1, 256], F32)
    coefb = nc.alloc_sbuf_tensor(f"cb{tag}", [128, 2 * 256], F32)
    ones1f = nc.alloc_sbuf_tensor(f"onf{tag}", [1, 128], F32)
    normw = nc.alloc_sbuf_tensor(f"nw{tag}", [16, 128], F32)
    normc = nc.alloc_sbuf_tensor(f"nc{tag}", [128, 16], F32)
    ssb = nc.alloc_sbuf_tensor(f"ss{tag}", [128, 16], F32)
    junk = nc.alloc_sbuf_tensor(f"jk{tag}", [128, 128], BF16)
    coef_dram = nc.dram_tensor(f"cscr{tag}", [2, 16, 16], F32, kind="Internal").ap()

    pq = [nc.alloc_psum_tensor(f"pq{tag}{i}", [128, 512], F32) for i in range(2)]
    pm = nc.alloc_psum_tensor(f"pm{tag}", [128, 512], F32)
    pm_o = nc.alloc_psum_tensor(f"pmo{tag}", [128, 128], F32)
    pm_a = nc.alloc_psum_tensor(f"pma{tag}", [128, 128], F32)
    pq_rel = [0, 0]

    for t in range(NTILE):
        p.dma("sync", q_sb[:, t * 2048:(t + 1) * 2048],
              qd[t * 128:(t + 1) * 128, :], s_x)
        p.dma("sync", k_sb[:, t * 2048:(t + 1) * 2048],
              kd_[t * 128:(t + 1) * 128, :], s_x)
        p.dma("sync", v_sb[:, t * 2048:(t + 1) * 2048],
              vd[t * 128:(t + 1) * 128, :], s_x)
    qkv_loaded = s_x.count
    p.dma("sync", dmat[:], dmat_d[:], s_m)
    p.dma("sync", ip1[:], ip1_d[:], s_m)
    p.dma("sync", ldrow[:], ld_dram[None, :], s_m)
    p.dma("sync", ecoef[:], ecoef_d.rearrange("a b -> (a b)")[None, :], s_m)
    p.dma("sync", normw[:], norm_w.rearrange("(ic p) -> ic p", p=128), s_m)
    const_in = s_m.count
    p.emit("vector", lambda e: e.memset(ones1f[:], 1.0), s_v)
    ones_ready = s_v.count
    if level == 10:
        _finish_passthrough(p, nc, x_dram, xo_dram, None, s_x, s_v, s_out,
                            (s_m, const_in), extra_mm=(s_x, qkv_loaded))
        return

    p.wait("tensor", s_m, const_in)
    p.wait("tensor", s_v, ones_ready)
    p.emit("tensor", lambda e: e.matmul(pm[0:128, 0:16], ones1f[:], ldrow[:],
                                        start=True, stop=True), s_mm)
    p.wait("vector", s_mm, s_mm.count)
    p.emit("vector", lambda e: e.tensor_copy(ldb[:], pm[0:128, 0:16]), s_v)
    p.wait("tensor", s_v, s_v.count)
    p.emit("tensor", lambda e: e.matmul(pm[0:128, 16:32], ip1[:, 0:128], ldrow[:],
                                        start=True, stop=True), s_mm)
    p.wait("scalar", s_mm, s_mm.count)
    p.emit("scalar", lambda e: e.activation(out=qdec[:], in_=pm[0:128, 16:32],
                                            func=AF.Exp), s_sc)
    p.wait("vector", s_sc, s_sc.count)
    p.emit("vector", lambda e: e.tensor_scalar(out=lamb[:], in0=ldb[:], scalar1=128.0,
                                               scalar2=None, op0=ALU.mult,
                                               op1=ALU.bypass), s_v)
    p.wait("scalar", s_v, s_v.count)
    p.emit("scalar", lambda e: e.activation(out=lamb[:], in_=lamb[:], func=AF.Exp),
           s_sc)
    p.wait("tensor", s_sc, s_sc.count)
    p.emit("tensor", lambda e: e.matmul(pm[0:128, 32:48], ip1[:, 128:256], ldrow[:],
                                        start=True, stop=True), s_mm)
    p.wait("scalar", s_mm, s_mm.count)
    p.emit("scalar", lambda e: e.activation(out=kdec[:], in_=pm[0:128, 32:48],
                                            func=AF.Exp), s_sc)
    tabs_ready = s_sc.count
    p.dbg_handles = {"ldb": ldb, "qdec": qdec, "kdec": kdec, "lamb": lamb,
                     "D_sb": D_sb, "coefb": coefb, "S_f": S_f, "dmat": dmat,
                     "ldrow": ldrow, "kqfm": kqfm, "sc_T": sc_T}
    if level == 11:
        _finish_passthrough(p, nc, x_dram, xo_dram, None, s_x, s_v, s_out,
                            (s_sc, tabs_ready))
        return
    p.wait("vector", s_sc, tabs_ready)
    p.wait("vector", s_m, const_in)
    for h in range(16):
        p.emit("vector", lambda e, o=D_sb[:, h * 128:(h + 1) * 128],
               s2=ldb[:, h:h + 1]: e.tensor_scalar(out=o, in0=dmat[:], scalar1=s2,
                                                   scalar2=None, op0=ALU.mult,
                                                   op1=ALU.bypass), s_v)
    p.wait("scalar", s_v, s_v.count)
    p.emit("scalar", lambda e: e.activation(out=D_sb[:], in_=D_sb[:], func=AF.Exp),
           s_sc)
    D_ready = s_sc.count
    if level == 12:
        _finish_passthrough(p, nc, x_dram, xo_dram, None, s_x, s_v, s_out,
                            (s_sc, D_ready))
        return
    for sl in range(2):
        p.wait("tensor", s_sc, s_sc.count)
        p.wait("tensor", s_v, s_v.count)
        p.emit("tensor", lambda e, r=ecoef[:, sl * 16:(sl + 1) * 16]:
               e.matmul(pm[0:16, 48:64], ldrow[:], r, start=True, stop=True), s_mm)
        p.wait("scalar", s_mm, s_mm.count)
        p.emit("scalar", lambda e: e.activation(out=coefT[:], in_=pm[0:16, 48:64],
                                                func=AF.Exp), s_sc)
        p.wait("scalar", s_sc, s_sc.count)
        p.dma("scalar", coef_dram[sl], coefT[:], s_m)
        p.wait("sync", s_m, s_m.count)
        p.dma("sync", coefrow[:], coef_dram[sl].rearrange("a b -> (a b)")[None, :],
              s_m)
        p.wait("tensor", s_m, s_m.count)
        p.wait("tensor", s_v, s_v.count)
        p.emit("tensor", lambda e: e.matmul(pm[0:128, 64:320], ones1f[:],
                                            coefrow[:], start=True, stop=True), s_mm)
        p.wait("vector", s_mm, s_mm.count)
        p.emit("vector", lambda e, o=coefb[:, sl * 256:(sl + 1) * 256]:
               e.tensor_copy(o, pm[0:128, 64:320]), s_v)
    coefb_ready = s_v.count
    if level == 13:
        _finish_passthrough(p, nc, x_dram, xo_dram, None, s_x, s_v, s_out,
                            (s_v, coefb_ready))
        return

    p.wait("tensor", s_m, const_in)
    p.wait("tensor", s_v, coefb_ready)
    p.emit("tensor", lambda e: e.transpose(pm[0:128, 384:400], normw[0:16, 0:128],
                                           ident32_sb[0:16, 0:16]), s_mm)
    p.wait("vector", s_mm, s_mm.count)
    p.emit("vector", lambda e: e.tensor_copy(normc[:], pm[0:128, 384:400]), s_v)
    normc_ready = s_v.count

    if level == 0:
        _finish_passthrough(p, nc, x_dram, xo_dram, x_sb, s_x, s_v, s_out,
                            (s_v, normc_ready))
        return
    kq_rel = 0
    o_rel = 0
    S_rel = normc_ready
    if level in (20, 21, 22):
        p.emit("vector", lambda e: e.memset(o_sb[:], 0.0), s_v)
        p.wait("vector", s_v)
    _hs = [0] if level in (20, 21, 22) else range(16)
    _sls = (lambda: [0]) if level in (20, 21) else (lambda: range(2))
    _cs = (lambda: [0]) if level == 20 else (lambda: range(4))
    for h in _hs:
        p.wait("sync", s_v, S_rel)
        p.dma("sync", Mh[:].rearrange("p (s e) -> p s e", s=16),
              m_all[:, h].rearrange("s p e -> p s e"), s_m)
        mh_loaded = s_m.count
        for sl in _sls():
            p.wait("vector", s_m, mh_loaded)
            p.wait("vector", s_v, coefb_ready)
            p.emit("vector", lambda e: e.memset(S_f[:], 0.0), s_v)
            p.wait("vector", s_v)
            for sg in range(16):
                p.emit("vector", lambda e, m=Mh[:, sg * 128:(sg + 1) * 128],
                       c=coefb[:, sl * 256 + h * 16 + sg: sl * 256 + h * 16 + sg + 1]:
                       e.scalar_tensor_tensor(out=S_f[:], in0=m, scalar=c, in1=S_f[:],
                                              op0=ALU.mult, op1=ALU.add), s_v)
                p.wait("vector", s_v)
            S_rel = s_v.count
            p.emit("vector", lambda e: e.tensor_copy(S_bf[:], S_f[:]), s_v)
            p.wait("vector", s_v)
            for c in _cs():
                t = sl * 4 + c
                kv = k_sb[:, t * 2048 + h * 128: t * 2048 + (h + 1) * 128]
                qv = q_sb[:, t * 2048 + h * 128: t * 2048 + (h + 1) * 128]
                vv = v_sb[:, t * 2048 + h * 128: t * 2048 + (h + 1) * 128]
                p.wait("vector", s_x, qkv_loaded)
                p.wait("vector", s_mm, kq_rel)
                kdq_done = p.emit("vector", lambda e, o=kdk, a=qv,
                                  s2=qdec[:, h:h + 1]:
                                  e.tensor_scalar(out=o[:], in0=a, scalar1=s2,
                                                  scalar2=None, op0=ALU.mult,
                                                  op1=ALU.bypass), s_v)
                tp.group([kv, qv, kdk[:]],
                         [kqfm[:, 0:128], kqfm[:, 128:256], kqfm[:, 256:384]],
                         src_ready=(s_v, kdq_done))
                kqfm_ready = tp.s_ev.count
                p.wait("tensor", tp.s_ev, kqfm_ready)
                p.wait("tensor", s_v, S_rel)
                p.emit("tensor", lambda e: e.matmul(pm[:, 0:128], kqfm[:, 0:128],
                                                    kqfm[:, 128:256], start=True,
                                                    stop=True), s_mm)
                scr_done = s_mm.count
                p.wait("vector", s_mm, scr_done)
                p.wait("vector", s_sc, D_ready)
                scT_done = p.emit("vector", lambda e, d=D_sb[:, h * 128:(h + 1) * 128]:
                                  e.tensor_tensor(out=sc_T[:], in0=pm[:, 0:128],
                                                  in1=d, op=ALU.mult), s_v)
                kdk_done = p.emit("vector", lambda e, o=kdk, a=kv,
                                  s2=kdec[:, h:h + 1]:
                                  e.tensor_scalar(out=o[:], in0=a, scalar1=s2,
                                                  scalar2=None, op0=ALU.mult,
                                                  op1=ALU.bypass), s_v)
                p.wait("tensor", s_v, scT_done)
                p.wait("tensor", s_v, o_rel)
                p.emit("tensor", lambda e, r=vv: e.matmul(pm_o[:], sc_T[:], r,
                                                          start=True, stop=False))
                p.emit("tensor", lambda e: e.matmul(pm_o[:], kqfm[:, 256:384],
                                                    S_bf[:], start=False, stop=True),
                       s_mm)
                o_done = s_mm.count
                p.wait("tensor", s_v, kdk_done)
                p.emit("tensor", lambda e, r=vv: e.matmul(pm_a[:], kdk[:], r,
                                                          start=True, stop=True), s_mm)
                kq_rel = s_mm.count
                a_done = s_mm.count
                p.wait("vector", s_mm, a_done)
                o_rel = p.emit("vector", lambda e,
                               o=o_sb[:, t * 2048 + h * 128: t * 2048 + (h + 1) * 128]:
                               e.tensor_copy(o, pm_o[:]), s_v)
                if dbg is not None and "oc" in dbg and level == 21:
                    p.wait("scalar", s_v, o_rel)
                    p.dma("scalar", dbg["oc"][c], 
                          o_sb[:, t * 2048 + h * 128: t * 2048 + (h + 1) * 128],
                          s_out)
                    p.wait("vector", s_out, s_out.count)
                p.emit("vector", lambda e, lm=lamb[:, h:h + 1]:
                       e.scalar_tensor_tensor(out=S_f[:], in0=S_f[:], scalar=lm,
                                              in1=pm_a[:], op0=ALU.mult,
                                              op1=ALU.add), s_v)
                p.wait("vector", s_v)
                p.emit("vector", lambda e: e.tensor_copy(S_bf[:], S_f[:]), s_v)
                S_rel = s_v.count
                p.wait("vector", s_v)
    o_ready = s_v.count
    main_done_mm = s_mm.count
    main_done_v = s_v.count
    if dbg is not None and "o" in dbg:
        p.wait("scalar", s_v, o_ready)
        for t in range(1 if level == 20 else 4 if level == 21 else NTILE):
            p.dma("scalar", dbg["o"][t * 128:(t + 1) * 128, :],
                  o_sb[:, t * 2048:(t + 1) * 2048], s_out)
        p.wait("vector", s_out, s_out.count)
    if level == 1:
        _finish_passthrough(p, nc, x_dram, xo_dram, None, s_x, s_v, s_out,
                            (s_v, o_ready), extra_mm=(s_mm, main_done_mm))
        return

    g_rel = 0
    for t in range(NTILE):
        p.wait("sync", s_v, g_rel)
        p.dma("sync", g_sb[:], gd[t * 128:(t + 1) * 128, :], s_g)
        g_loaded = s_g.count
        p.wait("scalar", s_g, g_loaded)
        p.wait("scalar", s_v, o_ready)
        sig_done = p.emit("scalar", lambda e: e.activation(out=g_sb[:], in_=g_sb[:],
                                                           func=AF.Sigmoid), s_sc)
        p.wait("vector", s_sc, sig_done)
        ov = o_sb[:, t * 2048:(t + 1) * 2048]
        p.emit("vector", lambda e, o=ov: e.tensor_tensor(out=o, in0=o, in1=g_sb[:],
                                                         op=ALU.mult), s_v)
        gated_done = s_v.count
        g_rel = gated_done
        p.wait("scalar", s_v, gated_done)
        for h in range(16):
            p.emit("scalar", lambda e, i=ov[:, h * 128:(h + 1) * 128],
                   z=ssb[:, h:h + 1]: e.activation(out=junk[:], in_=i,
                                                   func=AF.Square, accum_out=z),
                   s_sc)
            p.wait("scalar", s_sc, s_sc.count)
        sq_done = s_sc.count
        p.wait("vector", s_sc, sq_done)
        p.emit("vector", lambda e: e.tensor_scalar(out=ssb[:], in0=ssb[:],
                                                   scalar1=1.0 / 128.0, scalar2=1e-5,
                                                   op0=ALU.mult, op1=ALU.add), s_v)
        p.wait("scalar", s_v, s_v.count)
        p.emit("scalar", lambda e: e.activation(out=ssb[:], in_=ssb[:], func=AF.Ln),
               s_sc)
        p.wait("scalar", s_sc, s_sc.count)
        rs_done = p.emit("scalar", lambda e: e.activation(out=ssb[:], in_=ssb[:],
                                                          func=AF.Exp, scale=-0.5),
                         s_sc)
        p.wait("vector", s_sc, rs_done)
        for h in range(16):
            p.emit("vector", lambda e, o=ov[:, h * 128:(h + 1) * 128],
                   s2=ssb[:, h:h + 1]: e.tensor_scalar(out=o, in0=o, scalar1=s2,
                                                       scalar2=None, op0=ALU.mult,
                                                       op1=ALU.bypass),
                   s_v if h == 15 else None)
    o_normed = s_v.count

    if level == 2:
        _finish_passthrough(p, nc, x_dram, xo_dram, None, s_x, s_v, s_out,
                            (s_v, o_normed))
        return
    o_fm = q_sb
    for t in range(NTILE):
        for g in range(4):
            srcs = [o_sb[:, t * 2048 + (g * 4 + j) * 128:
                         t * 2048 + (g * 4 + j + 1) * 128] for j in range(4)]
            dsts = [o_fm[:, (g * 4 + j) * 1024 + t * 128:
                         (g * 4 + j) * 1024 + (t + 1) * 128] for j in range(4)]
            tp.group(srcs, dsts, src_ready=(s_v, o_normed))
    ofm_ready = tp.s_ev.count

    if level == 3:
        _finish_passthrough(p, nc, x_dram, xo_dram, None, s_x, s_v, s_out,
                            (tp.s_ev, ofm_ready))
        return
    kv_free_mm = main_done_mm
    kv_free_v = main_done_v
    p.wait("sync", s_mm, kv_free_mm)
    p.wait("sync", s_v, kv_free_v)
    for t in range(NTILE):
        p.dma("sync", x_sb[:, t * 2048:(t + 1) * 2048],
              x_dram[t * 128:(t + 1) * 128, :], s_x)
    x_all = s_x.count
    wo_s = Stream(p, 2)
    wo_rel = {0: 0, 1: 0}
    for oc in range(4):
        wb = oc % 2
        wo_loaded = wo_s.dma(
            "gpsimd", wo_sb[:, wb * 8192:(wb + 1) * 8192].rearrange(
                "p (ic f) -> p ic f", ic=16),
            wo[:, oc * 512:(oc + 1) * 512].rearrange("(ic p) f -> p ic f", p=128),
            gate=(s_mm, wo_rel[wb]), slot=wb)
        p.wait("vector", wo_loaded[0], wo_loaded[1])
        p.wait("vector", s_v, normc_ready)
        for ic in range(16):
            p.emit("vector", lambda e, o=wo_sb[:, wb * 8192 + ic * 512:
                                               wb * 8192 + (ic + 1) * 512],
                   r=normc[:, ic:ic + 1]: e.tensor_scalar(out=o, in0=o, scalar1=r,
                                                          scalar2=None, op0=ALU.mult,
                                                          op1=ALU.bypass),
                   s_v if ic == 15 else None)
        wo_scaled = s_v.count
        for t in range(NTILE):
            bank = pq[t % 2]
            p.wait("tensor", s_v, pq_rel[t % 2])
            p.wait("tensor", s_v, wo_scaled)
            if oc == 0 and t == 0:
                p.wait("tensor", tp.s_ev, ofm_ready)
            for ic in range(16):
                lhs = o_fm[:, ic * 1024 + t * 128: ic * 1024 + (t + 1) * 128]
                rhs = wo_sb[:, wb * 8192 + ic * 512: wb * 8192 + (ic + 1) * 512]
                p.emit("tensor", lambda e, o=bank, l=lhs, r=rhs,
                       st=(ic == 0), sp=(ic == 15):
                       e.matmul(o[:], l, r, start=st, stop=sp),
                       s_mm if ic == 15 else None)
            if t == NTILE - 1:
                wo_rel[wb] = s_mm.count
            prj_done = s_mm.count
            xv = x_sb[:, t * 2048 + oc * 512: t * 2048 + (oc + 1) * 512]
            p.wait("vector", s_x, x_all)
            p.wait("vector", s_mm, prj_done)
            pq_rel[t % 2] = p.emit("vector", lambda e, o=xv, i=bank:
                                   e.tensor_tensor(out=o, in0=o, in1=i[:],
                                                   op=ALU.add), s_v)
            if oc == 3:
                p.wait("scalar", s_v, s_v.count)
                p.dma("scalar", xo_dram[t * 128:(t + 1) * 128, :],
                      x_sb[:, t * 2048:(t + 1) * 2048], s_out)
    p.wait("scalar", s_out, s_out.count)


def build_ffn0_prog():
    nc = bass.Bass("TRN2", target_bir_lowering=False, debug=False)
    x = nc.dram_tensor("x", [T_CORE, D], F32, kind="ExternalInput").ap()
    wg = nc.dram_tensor("wg", [8192, D], F32, kind="ExternalInput").ap()
    wu = nc.dram_tensor("wu", [8192, D], F32, kind="ExternalInput").ap()
    wd = nc.dram_tensor("wd", [D, 8192], F32, kind="ExternalInput").ap()
    xo = nc.dram_tensor("xo", [T_CORE, D], F32, kind="ExternalOutput").ap()
    ident_d = nc.inline_tensor(np.eye(128, dtype=BF), "identc").ap()
    p = Prog(nc)
    ident_sb = nc.alloc_sbuf_tensor("ident", [128, 128], BF16)
    s_i = p.sem()
    p.dma("sync", ident_sb[:], ident_d[:], s_i)
    p.wait("tensor", s_i, 16)
    emit_ffn0(p, nc, x, xo, wg, wu, wd, ident_sb, tag="f0")
    p.build_block()
    return nc


def build_moe_prog(debug=False):
    nc = bass.Bass("TRN2", target_bir_lowering=False, debug=False)
    x = nc.dram_tensor("x", [T_CORE, D], F32, kind="ExternalInput").ap()
    W = {
        "sw1": nc.dram_tensor("sw1", [2, 2048, 1408], F32, kind="ExternalInput").ap(),
        "sb1": nc.dram_tensor("sb1", [2, 1408], F32, kind="ExternalInput").ap(),
        "sw2": nc.dram_tensor("sw2", [2, 1408, 2048], F32, kind="ExternalInput").ap(),
        "sb2": nc.dram_tensor("sb2", [2, 2048], F32, kind="ExternalInput").ap(),
        "rw1": nc.dram_tensor("rw1", [16, 2048, 1408], F32, kind="ExternalInput").ap(),
        "rb1": nc.dram_tensor("rb1", [16, 1408], F32, kind="ExternalInput").ap(),
        "rw2": nc.dram_tensor("rw2", [16, 1408, 2048], F32, kind="ExternalInput").ap(),
        "rb2": nc.dram_tensor("rb2", [16, 2048], F32, kind="ExternalInput").ap(),
        "cent": nc.dram_tensor("cent", [16, 2048], F32, kind="ExternalInput").ap(),
        "bias": nc.dram_tensor("bias", [16], F32, kind="ExternalInput").ap(),
    }
    xo = nc.dram_tensor("xo", [T_CORE, D], F32, kind="ExternalOutput").ap()
    ident_d = nc.inline_tensor(np.eye(128, dtype=BF), "identc").ap()
    ident32_d = nc.inline_tensor(np.eye(128, dtype=np.float32), "ident32c").ap()
    dbg = None
    if debug:
        dbg = {"g": nc.dram_tensor("gdbg", [T_CORE, 16], F32, kind="ExternalOutput").ap(),
               "u": nc.dram_tensor("udbg", [T_CORE, D], F32, kind="ExternalOutput").ap()}
    p = Prog(nc)
    ident_sb = nc.alloc_sbuf_tensor("ident", [128, 128], BF16)
    ident32_sb = nc.alloc_sbuf_tensor("ident32", [128, 128], F32)
    s_i = p.sem()
    p.dma("sync", ident_sb[:], ident_d[:], s_i)
    p.dma("sync", ident32_sb[:], ident32_d[:], s_i)
    p.wait("tensor", s_i, 32)
    emit_moe(p, nc, x, xo, W, ident_sb, ident32_sb[:], tag="m0", dbg=dbg)
    p.build_block()
    return nc


def shard_inputs_moe(inputs, x_full):
    names = {"sw1": "L1_ffn_shared_w1", "sb1": "L1_ffn_shared_b1",
             "sw2": "L1_ffn_shared_w2", "sb2": "L1_ffn_shared_b2",
             "rw1": "L1_ffn_routed_w1", "rb1": "L1_ffn_routed_b1",
             "rw2": "L1_ffn_routed_w2", "rb2": "L1_ffn_routed_b2",
             "cent": "L1_ffn_centroids", "bias": "L1_ffn_bias"}
    Ws = {k: np.ascontiguousarray(inputs[v]) for k, v in names.items()}
    maps = []
    for c in range(NCORES):
        a, b = core_strips(c)
        xs = np.concatenate([x_full[a * 512:(a + 1) * 512],
                             x_full[b * 512:(b + 1) * 512]], axis=0)
        maps.append({"x": xs, **Ws})
    return maps


def shard_inputs_ffn0(inputs):
    x = np.ascontiguousarray(inputs["L0_attn_x"].reshape(8192, 2048))
    wg = np.ascontiguousarray(inputs["L0_ffn_w_gate"])
    wu = np.ascontiguousarray(inputs["L0_ffn_w_up"])
    wd = np.ascontiguousarray(inputs["L0_ffn_w_down"])
    maps = []
    for c in range(NCORES):
        a, b = core_strips(c)
        xs = np.concatenate([x[a * 512:(a + 1) * 512], x[b * 512:(b + 1) * 512]], axis=0)
        maps.append({"x": xs, "wg": wg, "wu": wu, "wd": wd})
    return maps


def unshard_tokens(parts):
    out = np.empty((8192, parts[0].shape[1]), dtype=parts[0].dtype)
    for c in range(NCORES):
        a, b = core_strips(c)
        out[a * 512:(a + 1) * 512] = parts[c][:512]
        out[b * 512:(b + 1) * 512] = parts[c][512:]
    return out


def build_attn0_qkv_prog():
    nc = bass.Bass("TRN2", target_bir_lowering=False, debug=False)
    x = nc.dram_tensor("x", [T_CORE, D], F32, kind="ExternalInput").ap()
    wq = nc.dram_tensor("wq", [D, 4096], F32, kind="ExternalInput").ap()
    wk = nc.dram_tensor("wk", [D, 1024], F32, kind="ExternalInput").ap()
    wv = nc.dram_tensor("wv", [D, 512], F32, kind="ExternalInput").ap()
    cosd = nc.dram_tensor("cosd", [T_CORE, 128], F32, kind="ExternalInput").ap()
    sind = nc.dram_tensor("sind", [T_CORE, 128], F32, kind="ExternalInput").ap()
    qout = nc.dram_tensor("qout", [T_CORE, 4096], BF16, kind="ExternalOutput").ap()
    kfm = nc.dram_tensor("kfm", [1024, T_CORE], BF16, kind="ExternalOutput").ap()
    vout = nc.dram_tensor("vout", [T_CORE, 512], BF16, kind="ExternalOutput").ap()
    ident_d = nc.inline_tensor(np.eye(128, dtype=BF), "identc").ap()
    p = Prog(nc)
    ident_sb = nc.alloc_sbuf_tensor("ident", [128, 128], BF16)
    s_i = p.sem()
    p.dma("sync", ident_sb[:], ident_d[:], s_i)
    p.wait("tensor", s_i, 16)
    emit_attn0_qkv(p, nc, x, wq, wk, wv, cosd, sind, qout, kfm, vout,
                   ident_sb, tag="qk")
    p.build_block()
    return nc


def rope_tables(core):
    pos = np.array(strip_positions(core), dtype=np.float64)
    j = np.arange(128, dtype=np.float64)
    theta = 1.0 / 10000.0 ** (2.0 * j / 256.0)
    ang = pos[:, None] * theta[None, :]
    return np.cos(ang).astype(np.float32), np.sin(ang).astype(np.float32)


def strip_positions(core):
    a, b = core_strips(core)
    return list(range(a * 512, (a + 1) * 512)) + list(range(b * 512, (b + 1) * 512))


def shard_inputs_attn0_qkv(inputs, x_full):
    wq = np.ascontiguousarray(inputs["L0_attn_Wq"])
    wk = np.ascontiguousarray(inputs["L0_attn_Wk"])
    wv = np.ascontiguousarray(inputs["L0_attn_Wv"])
    maps = []
    for c in range(NCORES):
        a, b = core_strips(c)
        xs = np.concatenate([x_full[a * 512:(a + 1) * 512],
                             x_full[b * 512:(b + 1) * 512]], axis=0)
        cosd, sind = rope_tables(c)
        maps.append({"x": xs, "wq": wq, "wk": wk, "wv": wv,
                     "cosd": cosd, "sind": sind})
    return maps


def build_attn0_core_prog(core):
    raise NotImplementedError


def unshard_strip_tokens(parts):
    return unshard_tokens([np.asarray(x) for x in parts])


def attn0_masks():
    m = np.zeros((128, 4 * 512), dtype=np.float32)
    for qt in range(4):
        blk = np.full((128, 512), -1e30, dtype=np.float32)
        blk[:, :qt * 128] = 0.0
        tri = np.where(np.arange(128)[None, :] <= np.arange(128)[:, None], 0.0, -1e30)
        blk[:, qt * 128:(qt + 1) * 128] = tri
        m[:, qt * 512:(qt + 1) * 512] = blk
    return m


def build_attn0_heads_prog():
    nc = bass.Bass("TRN2", target_bir_lowering=False, debug=False)
    q = nc.dram_tensor("q", [8192, 512], BF16, kind="ExternalInput").ap()
    kg = nc.dram_tensor("kg", [256, 8192], BF16, kind="ExternalInput").ap()
    vg = nc.dram_tensor("vg", [8192, 128], BF16, kind="ExternalInput").ap()
    lq1 = nc.dram_tensor("lq1", [2, 128], F32, kind="ExternalInput").ap()
    lq2 = nc.dram_tensor("lq2", [2, 128], F32, kind="ExternalInput").ap()
    lk1 = nc.dram_tensor("lk1", [2, 128], F32, kind="ExternalInput").ap()
    lk2 = nc.dram_tensor("lk2", [2, 128], F32, kind="ExternalInput").ap()
    att = nc.dram_tensor("att", [8192, 256], BF16, kind="ExternalOutput").ap()
    masks_d = nc.inline_tensor(attn0_masks(), "masksc").ap()
    ident_d = nc.inline_tensor(np.eye(128, dtype=BF), "identc").ap()
    ident32_d = nc.inline_tensor(np.eye(128, dtype=np.float32), "ident32c").ap()
    p = Prog(nc)
    ident_sb = nc.alloc_sbuf_tensor("ident", [128, 128], BF16)
    ident32_sb = nc.alloc_sbuf_tensor("ident32", [128, 128], F32)
    s_i = p.sem()
    p.dma("sync", ident_sb[:], ident_d[:], s_i)
    p.dma("sync", ident32_sb[:], ident32_d[:], s_i)
    p.wait("tensor", s_i, 32)
    emit_attn0_heads(p, nc, q, kg, vg, lq1, lq2, lk1, lk2, att, masks_d,
                     ident_sb, ident32_sb[:], tag="ah")
    p.build_block()
    return nc


def shard_inputs_attn0_heads(inputs, q_all, k_glob, v_glob):
    maps = []
    for c in range(NCORES):
        kvh = c // 2
        maps.append({
            "q": np.ascontiguousarray(q_all[:, 2 * c * 256:(2 * c + 2) * 256]),
            "kg": np.ascontiguousarray(k_glob[kvh * 256:(kvh + 1) * 256, :]),
            "vg": np.ascontiguousarray(v_glob[:, kvh * 128:(kvh + 1) * 128]),
            "lq1": np.ascontiguousarray(inputs["L0_attn_lq1"][2 * c:2 * c + 2]),
            "lq2": np.ascontiguousarray(inputs["L0_attn_lq2"][2 * c:2 * c + 2]),
            "lk1": np.ascontiguousarray(inputs["L0_attn_lk1"][2 * c:2 * c + 2]),
            "lk2": np.ascontiguousarray(inputs["L0_attn_lk2"][2 * c:2 * c + 2]),
        })
    return maps


def build_attn0_out_prog():
    nc = bass.Bass("TRN2", target_bir_lowering=False, debug=False)
    x = nc.dram_tensor("x", [T_CORE, D], F32, kind="ExternalInput").ap()
    att = nc.dram_tensor("att", [T_CORE, D], BF16, kind="ExternalInput").ap()
    wo = nc.dram_tensor("wo", [D, D], F32, kind="ExternalInput").ap()
    rms_w = nc.dram_tensor("rms_w", [128], F32, kind="ExternalInput").ap()
    xo = nc.dram_tensor("xo", [T_CORE, D], F32, kind="ExternalOutput").ap()
    ident_d = nc.inline_tensor(np.eye(128, dtype=BF), "identc").ap()
    p = Prog(nc)
    ident_sb = nc.alloc_sbuf_tensor("ident", [128, 128], BF16)
    s_i = p.sem()
    p.dma("sync", ident_sb[:], ident_d[:], s_i)
    p.wait("tensor", s_i, 16)
    emit_attn0_out(p, nc, x, xo, att, wo, rms_w, ident_sb, tag="ao")
    p.build_block()
    return nc


def to_global_tokens(parts):
    return unshard_tokens([np.asarray(x) for x in parts])


def to_core_tokens(full, c):
    a, b = core_strips(c)
    return np.concatenate([full[a * 512:(a + 1) * 512],
                           full[b * 512:(b + 1) * 512]], axis=0)


def build_attn1_qkv_prog():
    nc = bass.Bass("TRN2", target_bir_lowering=False, debug=False)
    x = nc.dram_tensor("x", [T_CORE, D], F32, kind="ExternalInput").ap()
    wq = nc.dram_tensor("wq", [D, D], F32, kind="ExternalInput").ap()
    wk = nc.dram_tensor("wk", [D, D], F32, kind="ExternalInput").ap()
    wv = nc.dram_tensor("wv", [D, D], F32, kind="ExternalInput").ap()
    wg1 = nc.dram_tensor("wg1", [D, 128], F32, kind="ExternalInput").ap()
    wg2 = nc.dram_tensor("wg2", [128, D], F32, kind="ExternalInput").ap()
    ld = nc.dram_tensor("ld", [16], F32, kind="ExternalInput").ap()
    qo = nc.dram_tensor("qo", [T_CORE, D], BF16, kind="ExternalOutput").ap()
    ko = nc.dram_tensor("ko", [T_CORE, D], BF16, kind="ExternalOutput").ap()
    vo = nc.dram_tensor("vo", [T_CORE, D], BF16, kind="ExternalOutput").ap()
    go = nc.dram_tensor("go", [T_CORE, D], BF16, kind="ExternalOutput").ap()
    mout = nc.dram_tensor("mout", [2, 16, 128, 128], F32, kind="ExternalOutput").ap()
    colv = np.zeros((1, 512), dtype=np.float32)
    for c in range(4):
        colv[0, c * 128:(c + 1) * 128] = 511 - (c * 128 + np.arange(128))
    colv4_d = nc.inline_tensor(colv, "colv4c").ap()
    ident_d = nc.inline_tensor(np.eye(128, dtype=BF), "identc").ap()
    p = Prog(nc)
    ident_sb = nc.alloc_sbuf_tensor("ident", [128, 128], BF16)
    s_i = p.sem()
    p.dma("sync", ident_sb[:], ident_d[:], s_i)
    p.wait("tensor", s_i, 16)
    emit_attn1_qkv(p, nc, x, [wq, wk, wv], wg1, wg2, ld, qo, ko, vo, go, mout,
                   colv4_d, ident_sb, tag="a1")
    p.build_block()
    return nc


def shard_inputs_attn1_qkv(inputs, x_full):
    Ws = {"wq": np.ascontiguousarray(inputs["L1_attn_Wq"]),
          "wk": np.ascontiguousarray(inputs["L1_attn_Wk"]),
          "wv": np.ascontiguousarray(inputs["L1_attn_Wv"]),
          "wg1": np.ascontiguousarray(inputs["L1_attn_Wg1"]),
          "wg2": np.ascontiguousarray(inputs["L1_attn_Wg2"]),
          "ld": np.ascontiguousarray(inputs["L1_attn_log_decay"])}
    maps = []
    for c in range(NCORES):
        maps.append({"x": to_core_tokens(x_full, c), **Ws})
    return maps


def build_attn1_core_prog(level=5, debug=False):
    nc = bass.Bass("TRN2", target_bir_lowering=False, debug=False)
    x = nc.dram_tensor("x", [T_CORE, D], F32, kind="ExternalInput").ap()
    qd = nc.dram_tensor("qd", [T_CORE, D], BF16, kind="ExternalInput").ap()
    kd_ = nc.dram_tensor("kd", [T_CORE, D], BF16, kind="ExternalInput").ap()
    vd = nc.dram_tensor("vd", [T_CORE, D], BF16, kind="ExternalInput").ap()
    gd = nc.dram_tensor("gd", [T_CORE, D], BF16, kind="ExternalInput").ap()
    m_all = nc.dram_tensor("m_all", [16, 16, 128, 128], F32, kind="ExternalInput").ap()
    ld = nc.dram_tensor("ld", [16], F32, kind="ExternalInput").ap()
    wo = nc.dram_tensor("wo", [D, D], F32, kind="ExternalInput").ap()
    norm_w = nc.dram_tensor("norm_w", [D], F32, kind="ExternalInput").ap()
    ecoef = nc.dram_tensor("ecoef", [2, 16], F32, kind="ExternalInput").ap()
    xo = nc.dram_tensor("xo", [T_CORE, D], F32, kind="ExternalOutput").ap()
    dmat = np.zeros((128, 128), dtype=np.float32)
    for s in range(128):
        for cc in range(128):
            dmat[s, cc] = (cc - s) if cc >= s else 1e30
    dmat_d = nc.inline_tensor(dmat, "dmatc").ap()
    ip1 = np.zeros((1, 256), dtype=np.float32)
    ip1[0, 0:128] = np.arange(128) + 1
    ip1[0, 128:256] = 127 - np.arange(128)
    ip1_d = nc.inline_tensor(ip1, "ip1c").ap()
    ident_d = nc.inline_tensor(np.eye(128, dtype=BF), "identc").ap()
    ident32_d = nc.inline_tensor(np.eye(128, dtype=np.float32), "ident32c").ap()
    dbg = None
    if debug:
        dbg = {"o": nc.dram_tensor("odbg", [T_CORE, D], BF16, kind="ExternalOutput").ap(),
               "oc": nc.dram_tensor("ocdbg", [4, 128, 128], BF16, kind="ExternalOutput").ap()}
    p = Prog(nc)
    ident_sb = nc.alloc_sbuf_tensor("ident", [128, 128], BF16)
    ident32_sb = nc.alloc_sbuf_tensor("ident32", [128, 128], F32)
    s_i = p.sem()
    p.dma("sync", ident_sb[:], ident_d[:], s_i)
    p.dma("sync", ident32_sb[:], ident32_d[:], s_i)
    p.wait("tensor", s_i, 32)
    emit_attn1_core(p, nc, x, xo, qd, kd_, vd, gd, m_all, ld, wo, norm_w,
                    ecoef, dmat_d, ip1_d, ident_sb, ident32_sb[:], tag="b1",
                    level=level, dbg=dbg)
    p.build_block()
    nc._dbg_handles = getattr(p, "dbg_handles", None)
    return nc


def attn1_ecoef(core):
    e = np.zeros((2, 16), dtype=np.float32)
    for sl, s in enumerate(core_strips(core)):
        for sg in range(16):
            e[sl, sg] = 512.0 * (s - 1 - sg) if sg < s else 1e30
    return e


def shard_inputs_attn1_core(inputs, x_full, q_parts, k_parts, v_parts, g_parts,
                            m_parts):
    m_all = np.zeros((16, 16, 128, 128), dtype=np.float32)
    for c in range(NCORES):
        a, b = core_strips(c)
        m_all[a] = np.asarray(m_parts[c])[0]
        m_all[b] = np.asarray(m_parts[c])[1]
    Ws = {"ld": np.ascontiguousarray(inputs["L1_attn_log_decay"]),
          "wo": np.ascontiguousarray(inputs["L1_attn_Wo"]),
          "norm_w": np.ascontiguousarray(inputs["L1_attn_norm_w"]),
          "m_all": m_all}
    maps = []
    for c in range(NCORES):
        maps.append({"x": to_core_tokens(x_full, c),
                     "qd": np.asarray(q_parts[c]), "kd": np.asarray(k_parts[c]),
                     "vd": np.asarray(v_parts[c]), "gd": np.asarray(g_parts[c]),
                     "ecoef": attn1_ecoef(c), **Ws})
    return maps


_PROGS = {}

def _prog(name, builder):
    if name not in _PROGS:
        _PROGS[name] = builder()
    return _PROGS[name]


def kernel(**inputs):
    x = np.ascontiguousarray(np.asarray(inputs["L0_attn_x"],
                                        dtype=np.float32).reshape(8192, 2048))
    cores = list(range(NCORES))

    def run(nc, maps):
        return run_bass_kernel_spmd(nc, maps, core_ids=cores).results

    nc = _prog("ffn0", build_ffn0_prog)
    wg = np.ascontiguousarray(inputs["L0_ffn_w_gate"])
    wu = np.ascontiguousarray(inputs["L0_ffn_w_up"])
    wdn = np.ascontiguousarray(inputs["L0_ffn_w_down"])
    maps = [{"x": to_core_tokens(x, c), "wg": wg, "wu": wu, "wd": wdn}
            for c in cores]
    x = to_global_tokens([r["xo"] for r in run(nc, maps)])

    nc = _prog("a0qkv", build_attn0_qkv_prog)
    res = run(nc, shard_inputs_attn0_qkv(inputs, x))
    q_all = to_global_tokens([r["qout"] for r in res])
    v_glob = to_global_tokens([r["vout"] for r in res])
    k_glob = np.empty((1024, 8192), dtype=BF)
    for c in cores:
        a, b = core_strips(c)
        kfm = np.asarray(res[c]["kfm"])
        k_glob[:, a * 512:(a + 1) * 512] = kfm[:, 0:512]
        k_glob[:, b * 512:(b + 1) * 512] = kfm[:, 512:1024]

    nc = _prog("a0heads", build_attn0_heads_prog)
    res = run(nc, shard_inputs_attn0_heads(inputs, q_all, k_glob, v_glob))
    att_full = np.concatenate([np.asarray(r["att"]) for r in res], axis=1)

    nc = _prog("a0out", build_attn0_out_prog)
    wo0 = np.ascontiguousarray(inputs["L0_attn_Wo"])
    rmsw0 = np.ascontiguousarray(inputs["L0_attn_rms_w"])
    maps = [{"x": to_core_tokens(x, c), "att": to_core_tokens(att_full, c),
             "wo": wo0, "rms_w": rmsw0} for c in cores]
    x = to_global_tokens([r["xo"] for r in run(nc, maps)])

    nc = _prog("ffn0", build_ffn0_prog)
    maps = [{"x": to_core_tokens(x, c), "wg": wg, "wu": wu, "wd": wdn}
            for c in cores]
    x = to_global_tokens([r["xo"] for r in run(nc, maps)])

    nc = _prog("moe", build_moe_prog)
    x = to_global_tokens([r["xo"] for r in run(nc, shard_inputs_moe(inputs, x))])

    nc = _prog("a1qkv", build_attn1_qkv_prog)
    res = run(nc, shard_inputs_attn1_qkv(inputs, x))
    q_p = [np.asarray(r["qo"]) for r in res]
    k_p = [np.asarray(r["ko"]) for r in res]
    v_p = [np.asarray(r["vo"]) for r in res]
    g_p = [np.asarray(r["go"]) for r in res]
    m_p = [np.asarray(r["mout"]) for r in res]

    nc = _prog("a1core", build_attn1_core_prog)
    res = run(nc, shard_inputs_attn1_core(inputs, x, q_p, k_p, v_p, g_p, m_p))
    x = to_global_tokens([r["xo"] for r in res])

    nc = _prog("moe", build_moe_prog)
    x = to_global_tokens([r["xo"] for r in run(nc, shard_inputs_moe(inputs, x))])

    return np.ascontiguousarray(x.reshape(1, 8192, 2048))
